# version 06
# baseline (speedup 1.0000x reference)
import os
import numpy as np
import ml_dtypes
from contextlib import ExitStack

import concourse.bass as bass
import concourse.mybir as mybir
from concourse.bass_utils import run_bass_kernel_spmd

F32 = mybir.dt.float32
BF16 = mybir.dt.bfloat16
I32 = mybir.dt.int32
AF = mybir.ActivationFunctionType
ALU = mybir.AluOpType
AX = mybir.AxisListType
ET = mybir.EngineType

NCORE = 8
D = 4096
NIT = 32
TOK = 1024
TC = 512
INTER = 11008
NI = 86
HD = 128
S = 4096
RG = [[0, 1, 2, 3, 4, 5, 6, 7]]
BF = ml_dtypes.bfloat16

_NC_CACHE = {}


def _sb(x):
    return np.ascontiguousarray(x)


def make_host_constants():
    c = {}
    c["c_ident"] = np.eye(128, dtype=BF)
    R = np.zeros((128, 128), np.float32)
    for d in range(64):
        R[d, d + 64] = -1.0
        R[d + 64, d] = 1.0
    c["c_rott"] = _sb(R.T.astype(BF))
    m = np.where(np.tril(np.ones((128, 128), bool)), 0.0, -30000.0)
    c["c_mask"] = m.astype(np.float32)
    inv_freq = 1.0 / (10000.0 ** (np.arange(0, HD, 2, dtype=np.float64) / HD))
    freqs = np.arange(S, dtype=np.float64)[:, None] * inv_freq[None, :]
    emb = np.concatenate([freqs, freqs], axis=1)
    cos = np.cos(emb).T.astype(np.float32)
    sin = np.sin(emb).T.astype(np.float32)
    sc = 1.0 / np.sqrt(HD)
    c["c_cosq"] = _sb(cos * sc)
    c["c_sinq"] = _sb(sin * sc)
    c["c_cosk"] = _sb(cos)
    c["c_sink"] = _sb(sin)
    sh8 = np.zeros((128, 8), np.int32)
    for j in range(8):
        sh8[:, j] = 4 * j
    c["c_sh8"] = sh8
    shp = np.zeros((128, 4), np.int32)
    for mm_ in range(4):
        shp[0:64, mm_] = 8 * mm_
        shp[64:128, mm_] = 8 * mm_ + 4
    c["c_shp"] = shp
    c["c_m15"] = np.full((128, 1), 15, np.int32)
    return c


def shard_inputs(inputs):
    cst = make_host_constants()
    x = inputs["L0_attn_a_x"]
    W = _sb(np.asarray(inputs["L0_attn_a_weight"], np.float32))
    bias_t = _sb(np.asarray(inputs["L0_attn_a_bias"], np.float32).reshape(32, 128).T)
    sq = np.asarray(inputs["L0_attn_b_scales_q"], np.float32)
    sk = np.asarray(inputs["L0_attn_b_scales_k"], np.float32)
    sv = np.asarray(inputs["L0_attn_b_scales_v"], np.float32)
    so = np.asarray(inputs["L0_attn_b_scales_o"], np.float32)
    qq = np.asarray(inputs["L0_attn_b_qw_q"], np.int32)
    qk = np.asarray(inputs["L0_attn_b_qw_k"], np.int32)
    qv = np.asarray(inputs["L0_attn_b_qw_v"], np.int32)
    qo = np.asarray(inputs["L0_attn_b_qw_o"], np.int32)
    gq = np.asarray(inputs["L0_ffn_gate_qweight"], np.int32)
    gs = np.asarray(inputs["L0_ffn_gate_scales"], np.float32)
    gz = np.asarray(inputs["L0_ffn_gate_qzeros"], np.int32)
    uq = np.asarray(inputs["L0_ffn_up_qweight"], np.int32)
    us = np.asarray(inputs["L0_ffn_up_scales"], np.float32)
    uz = np.asarray(inputs["L0_ffn_up_qzeros"], np.int32)
    dq = np.asarray(inputs["L0_ffn_down_qweight"], np.int32)
    ds = np.asarray(inputs["L0_ffn_down_scales"], np.float32)
    dz = np.asarray(inputs["L0_ffn_down_qzeros"], np.int32)

    in_maps = []
    for c in range(NCORE):
        b, q4 = c // 4, c % 4
        im = dict(cst)
        im["xin"] = _sb(np.asarray(x[b, q4 * TOK:(q4 + 1) * TOK, :], np.float32).T)
        im["aw"] = W
        im["abias"] = bias_t
        im["qwq"] = _sb(qq[:, 512 * c: 512 * (c + 1)])
        im["qwk"] = _sb(qk[:, 128 * c: 128 * (c + 1)])
        im["qwv"] = _sb(qv[:, 128 * c: 128 * (c + 1)])
        im["sclq"] = _sb(sq[:, 512 * c: 512 * (c + 1)])
        im["sclk"] = _sb(sk[:, 128 * c: 128 * (c + 1)])
        im["sclv"] = _sb(sv[:, 128 * c: 128 * (c + 1)])
        im["qwo"] = _sb(qo[512 * c: 512 * (c + 1), :])
        im["sclo"] = _sb(so[4 * c: 4 * (c + 1), :])
        im["gqw"] = _sb(gq[64 * c: 64 * (c + 1), :])
        im["gsc"] = _sb(gs[4 * c: 4 * (c + 1), :])
        im["gzr"] = _sb(gz[4 * c: 4 * (c + 1), :])
        im["uqw"] = _sb(uq[64 * c: 64 * (c + 1), :])
        im["usc"] = _sb(us[4 * c: 4 * (c + 1), :])
        im["uzr"] = _sb(uz[4 * c: 4 * (c + 1), :])
        im["dqw"] = _sb(dq[:, 512 * c: 512 * (c + 1)])
        im["dsc"] = _sb(ds[:, 512 * c: 512 * (c + 1)])
        im["dzr"] = _sb(dz[:, 64 * c: 64 * (c + 1)])
        im["ggi"] = _sb(np.asarray(inputs["L0_ffn_gate_g_idx"], np.int32).reshape(1, -1))
        im["ugi"] = _sb(np.asarray(inputs["L0_ffn_up_g_idx"], np.int32).reshape(1, -1))
        im["dgi"] = _sb(np.asarray(inputs["L0_ffn_down_g_idx"], np.int32).reshape(1, -1))
        im["bidx"] = np.array([[b]], np.int32)
        im["qidx"] = np.array([[q4]], np.int32)
        in_maps.append(im)
    return in_maps


class KB:
    def __init__(self, nc, ctx):
        self.nc = nc
        self.ctx = ctx
        self.streams = {k: [] for k in ("sync", "tensor", "vector", "scalar", "gpsimd")}
        self.sems = {}
        self.cnt = {}
        self.bg_vector = []

    def sem(self, name):
        self.sems[name] = self.ctx.enter_context(self.nc.semaphore(name))
        self.cnt[name] = 0

    def tok(self, name, n):
        self.cnt[name] += n
        return self.cnt[name]

    def emit(self, stream, fn):
        self.streams[stream].append(fn)

    def wait(self, stream, sem, val):
        if val is None or val <= 0:
            return
        h = self.sems[sem]
        self.emit(stream, lambda e, h=h, v=val: e.wait_ge(h, v))

    def pump_bg(self, n):
        for _ in range(min(n, len(self.bg_vector))):
            self.emit("vector", self.bg_vector.pop(0))

    _sbuf_n = 0

    def sb(self, scope, name, shape, dtype):
        KB._sbuf_n += 1
        return scope.enter_context(self.nc.sbuf_tensor(f"{name}_{KB._sbuf_n}", shape, dtype))

    def barrier(self, exclude=(), flush_bg=True):
        if flush_bg:
            self.pump_bg(10 ** 9)
        snap = dict(self.cnt)
        for st in self.streams:
            for sname, v in snap.items():
                if sname in exclude or (not flush_bg and sname == "BG"):
                    continue
                self.wait(st, sname, v)


def dma(kb, stream, sem, out, in_, **kw):
    t = kb.tok(sem, 16)
    h = kb.sems[sem]
    kb.emit(stream, lambda e, o=out, i=in_, h=h, kw=kw: e.dma_start(out=o, in_=i, **kw).then_inc(h, 16))
    return t


def inc(kb, stream, sem, fn):
    t = kb.tok(sem, 1)
    h = kb.sems[sem]
    kb.emit(stream, lambda e, f=fn, h=h: f(e).then_inc(h, 1))
    return t


def op(kb, stream, fn):
    kb.emit(stream, fn)


def build_program(nphase=4, detect_races=False):
    nc = bass.Bass("TRN2", target_bir_lowering=False, debug=False,
                   detect_race_conditions=detect_races)

    def din(name, shape, dtype):
        return nc.dram_tensor(name, shape, dtype, kind="ExternalInput")

    xin = din("xin", [D, TOK], F32)
    aw = din("aw", [D, D], F32)
    abias = din("abias", [128, 32], F32)
    qwq = din("qwq", [D, 512], I32)
    qwk = din("qwk", [D, 128], I32)
    qwv = din("qwv", [D, 128], I32)
    sclq = din("sclq", [32, 512], F32)
    sclk = din("sclk", [32, 128], F32)
    sclv = din("sclv", [32, 128], F32)
    qwo = din("qwo", [512, D], I32)
    sclo = din("sclo", [4, D], F32)
    gqw = din("gqw", [64, INTER], I32)
    gsc = din("gsc", [4, INTER], F32)
    gzr = din("gzr", [4, 1376], I32)
    uqw = din("uqw", [64, INTER], I32)
    usc = din("usc", [4, INTER], F32)
    uzr = din("uzr", [4, 1376], I32)
    dqw = din("dqw", [1376, 512], I32)
    dsc = din("dsc", [86, 512], F32)
    dzr = din("dzr", [86, 64], I32)
    ggi = din("ggi", [1, D], I32)
    ugi = din("ugi", [1, D], I32)
    dgi = din("dgi", [1, INTER], I32)
    bidx = din("bidx", [1, 1], I32)
    qidx = din("qidx", [1, 1], I32)
    c_ident = din("c_ident", [128, 128], BF16)
    c_rott = din("c_rott", [128, 128], BF16)
    c_mask = din("c_mask", [128, 128], F32)
    c_cosq = din("c_cosq", [128, S], F32)
    c_sinq = din("c_sinq", [128, S], F32)
    c_cosk = din("c_cosk", [128, S], F32)
    c_sink = din("c_sink", [128, S], F32)
    c_sh8 = din("c_sh8", [128, 8], I32)
    c_shp = din("c_shp", [128, 4], I32)
    c_m15 = din("c_m15", [128, 1], I32)

    out_ext = nc.dram_tensor("out", [D, TOK], F32, kind="ExternalOutput")

    x_dram = nc.dram_tensor("x_dram", [D, TOK], F32)
    g_contrib = nc.dram_tensor("g_contrib", [512, INTER], BF16)
    u_contrib = nc.dram_tensor("u_contrib", [512, INTER], BF16)
    d_contrib = nc.dram_tensor("d_contrib", [INTER, 512], BF16)
    o_contrib = nc.dram_tensor("o_contrib", [512, D], BF16)
    gate_bf = nc.dram_tensor("gate_bf", [D, INTER], BF16)
    up_bf = nc.dram_tensor("up_bf", [D, INTER], BF16)
    down_bf = nc.dram_tensor("down_bf", [8 * INTER, 512], BF16)
    wo_bf = nc.dram_tensor("wo_bf", [D, D], BF16)
    wq_bf = nc.dram_tensor("wq_bf", [D, 512], BF16)
    wk_bf = nc.dram_tensor("wk_bf", [D, 128], BF16)
    wv_bf = nc.dram_tensor("wv_bf", [D, 128], BF16)
    zg_scr = nc.dram_tensor("zg_scr", [4, INTER], F32)
    zu_scr = nc.dram_tensor("zu_scr", [4, INTER], F32)
    zd_scr = nc.dram_tensor("zd_scr", [86, 512], F32)
    xn_contrib = [nc.dram_tensor(f"xn_contrib{i}", [D, TC], BF16) for i in range(2)]
    xn_ag = [nc.dram_tensor(f"xn_ag{i}", [8 * D, TC], BF16) for i in range(2)]
    q_scr = nc.dram_tensor("q_scr", [1024, S], BF16)
    k_scr = nc.dram_tensor("k_scr", [256, S], BF16)
    v_scr = nc.dram_tensor("v_scr", [256, S], BF16)
    ctx_contrib = nc.dram_tensor("ctx_contrib", [1024, S], BF16)
    ctx_ag = nc.dram_tensor("ctx_ag", [8 * 1024, S], BF16)
    kdbg = os.environ.get("KDBG")
    if kdbg:
        dbg_xn = nc.dram_tensor("dbg_xn", [128, NIT * TOK], BF16, kind="ExternalOutput")
        dbg_wt = nc.dram_tensor("dbg_wt", [128, NIT * 128], BF16, kind="ExternalOutput")
        dbg_ct = nc.dram_tensor("dbg_ct", [128, NIT * 1024], BF16, kind="ExternalOutput")
        dbg_g = nc.dram_tensor("dbg_g", [8 * INTER, 512], BF16, kind="ExternalOutput")
        dbg_z = nc.dram_tensor("dbg_z", [86, 512], F32, kind="ExternalOutput")
        dbg_zp = nc.dram_tensor("dbg_zp", [86, 64], I32, kind="ExternalOutput")
        dbg_zg = nc.dram_tensor("dbg_zg", [4, INTER], F32, kind="ExternalOutput")

    with ExitStack() as CTX:
        kb = KB(nc, CTX)
        for s_ in ("LD", "LW0", "LW1", "LU0", "LU1", "LDN0", "LDN1", "LX0", "LX1", "LA0", "LA1", "GD", "SD0", "SD1", "CC", "TE", "VE", "SE", "BG"):
            kb.sem(s_)

        pb = [CTX.enter_context(nc.psum_tensor(f"pb{i}", [128, 512], F32))
              for i in range(8)]

        ident = kb.sb(CTX, "ident", [128, 128], BF16)
        rott = kb.sb(CTX, "rott", [128, 128], BF16)
        maskt = kb.sb(CTX, "maskt", [128, 128], F32)
        biast = kb.sb(CTX, "biast", [128, 32], F32)
        onesc = kb.sb(CTX, "onesc", [128, 1], BF16)
        onesr = kb.sb(CTX, "onesr", [1, 128], F32)
        stats = kb.sb(CTX, "stats", [1, 1024], F32)
        sh8 = kb.sb(CTX, "sh8", [128, 8], I32)
        shp = kb.sb(CTX, "shp", [128, 4], I32)
        m15 = kb.sb(CTX, "m15", [128, 1], I32)
        mrep = kb.sb(CTX, "mrep", [128, 512], F32)
        rrep = kb.sb(CTX, "rrep", [128, 512], F32)
        nmax = kb.sb(CTX, "nmax", [128, 1], F32)
        rsum = kb.sb(CTX, "rsum", [128, 1], F32)
        rinv = kb.sb(CTX, "rinv", [128, 1], F32)
        dmt = kb.sb(CTX, "dmt", [128, 128], BF16)
        scal1 = kb.sb(CTX, "scal1", [1, 512], F32)
        ridb = kb.sb(CTX, "ridb", [1, 1], I32)
        ridq = kb.sb(CTX, "ridq", [1, 1], I32)
        gidx_t = kb.sb(CTX, "gidx_t", [1, 128], I32)

        t_const = 0
        for dst, src in ((ident, c_ident), (rott, c_rott), (maskt, c_mask),
                         (biast, abias), (sh8, c_sh8), (shp, c_shp), (m15, c_m15),
                         (ridb, bidx), (ridq, qidx)):
            t_const = dma(kb, "sync", "LD", dst[:], src[:])
        for gsrc in (ggi, ugi, dgi):
            t_const = dma(kb, "sync", "LD", gidx_t[0:1, 0:128], gsrc[0:1, 0:128])
        op(kb, "vector", lambda e: e.memset(onesc[:], 1.0))
        t_ones = inc(kb, "vector", "VE", lambda e: e.memset(onesr[:], 1.0))

        ln_prev = {"xn": 0, "ssq": 0, "mr": 0, "sq": 0}

        def emit_ln(scope, src_dram, tcc, xn, off, vec_war=0):
            xc = kb.sb(scope, "xc", [128, NIT * TC], F32)
            ba = kb.sb(scope, "ba", [128, NIT * TC], BF16)
            kb.wait("sync", "VE", ln_prev["xn"])
            kb.wait("sync", "SE", ln_prev["sq"])
            t_x = dma(kb, "sync", "LD", xc[:].rearrange("p (i t) -> p i t", t=TC),
                      src_dram[:, tcc * TC:(tcc + 1) * TC].rearrange("(i p) t -> p i t", p=128))
            kb.wait("scalar", "LD", t_x)
            kb.wait("scalar", "TE", ln_prev["ssq"])
            t_xb = inc(kb, "scalar", "SE", lambda e: e.activation(ba[:], xc[:], AF.Copy))
            kb.wait("tensor", "SE", t_xb)
            kb.wait("tensor", "VE", max(t_ones, ln_prev["mr"]))
            t_sum = None
            for it in range(NIT):
                t_sum = inc(kb, "tensor", "TE",
                            lambda e, it=it: e.matmul(
                                pb[6][0:1, :], onesc[:], ba[:, it * TC:(it + 1) * TC],
                                start=(it == 0), stop=(it == NIT - 1)))
            kb.wait("scalar", "TE", t_sum)
            t_sq = inc(kb, "scalar", "SE", lambda e: e.activation(ba[:], xc[:], AF.Square))
            kb.wait("tensor", "SE", t_sq)
            t_ssq = None
            for it in range(NIT):
                t_ssq = inc(kb, "tensor", "TE",
                            lambda e, it=it: e.matmul(
                                pb[7][0:1, :], onesc[:], ba[:, it * TC:(it + 1) * TC],
                                start=(it == 0), stop=(it == NIT - 1)))
            kb.wait("vector", "TE", t_ssq)
            op(kb, "vector", lambda e: e.tensor_scalar_mul(stats[0:1, 0:512], pb[6][0:1, :], 1.0 / D))
            op(kb, "vector", lambda e: e.tensor_scalar_mul(scal1[0:1, :], pb[7][0:1, :], 1.0 / D))
            op(kb, "vector", lambda e: e.tensor_tensor(stats[0:1, 512:1024],
                                                       stats[0:1, 0:512], stats[0:1, 0:512], ALU.mult))
            op(kb, "vector", lambda e: e.tensor_sub(scal1[0:1, :], scal1[0:1, :], stats[0:1, 512:1024]))
            t_var = inc(kb, "vector", "VE",
                        lambda e: e.tensor_scalar_add(scal1[0:1, :], scal1[0:1, :], 1e-5))
            kb.wait("scalar", "VE", t_var)
            op(kb, "scalar", lambda e: e.activation(scal1[0:1, :], scal1[0:1, :], AF.Ln))
            t_rstd = inc(kb, "scalar", "SE",
                         lambda e: e.activation(stats[0:1, 512:1024], scal1[0:1, :], AF.Exp, scale=-0.5))
            kb.wait("tensor", "SE", t_rstd)
            kb.wait("tensor", "VE", t_var)
            op(kb, "tensor", lambda e: e.matmul(pb[6][:, :], onesr[:], stats[0:1, 0:512],
                                                start=True, stop=True))
            t_rep = inc(kb, "tensor", "TE",
                        lambda e: e.matmul(pb[7][:, :], onesr[:], stats[0:1, 512:1024],
                                           start=True, stop=True))
            kb.wait("vector", "TE", t_rep)
            op(kb, "vector", lambda e: e.tensor_copy(mrep[:], pb[6][:, :]))
            t_mr = inc(kb, "vector", "VE", lambda e: e.tensor_copy(rrep[:], pb[7][:, :]))
            op(kb, "vector", lambda e: e.tensor_tensor(
                xc[:].rearrange("p (i t) -> p i t", t=TC),
                xc[:].rearrange("p (i t) -> p i t", t=TC),
                mrep[:].rearrange("p (a t) -> p a t", a=1).broadcast_to([128, NIT, TC]),
                ALU.subtract))
            kb.wait("vector", "TE", vec_war)
            t_xn = inc(kb, "vector", "VE", lambda e, off=off: e.tensor_tensor(
                xn[:, off:off + NIT * TC].rearrange("p (i t) -> p i t", t=TC),
                xc[:].rearrange("p (i t) -> p i t", t=TC),
                rrep[:].rearrange("p (a t) -> p a t", a=1).broadcast_to([128, NIT, TC]),
                ALU.mult))
            ln_prev.update(xn=t_xn, ssq=t_ssq, mr=t_mr, sq=t_sq)
            return t_xn

        W0 = 688
        p0 = ExitStack()
        dq_qw = kb.sb(p0, "dq_qw", [128, W0], I32)
        dq_w4 = kb.sb(p0, "dq_w4", [128, W0], I32)
        dq_f = kb.sb(p0, "dq_f", [128, W0], F32)
        dq_s = kb.sb(p0, "dq_s", [128, W0], F32)
        dq_z = kb.sb(p0, "dq_z", [128, W0], F32)
        dq_o = kb.sb(p0, "dq_o", [128, W0], BF16)
        dq_zp = kb.sb(p0, "dq_zp", [128, 86], I32)
        dq_zj = kb.sb(p0, "dq_zj", [128, 86], I32)
        dq_zi = kb.sb(p0, "dq_zi", [128, 688], I32)
        dq_zf = kb.sb(p0, "dq_zf", [128, 688], F32)

        def bg(fn):
            kb.bg_vector.append(fn)

        def bg_tok(fn):
            t = kb.tok("BG", 1)
            h = kb.sems["BG"]
            kb.bg_vector.append(lambda e, f=fn, h=h: f(e).then_inc(h, 1))
            return t

        def bg_wait(sem, t):
            if t:
                h = kb.sems[sem]
                kb.bg_vector.append(lambda e, h=h, v=t: e.wait_ge(h, v))

        ag_toks = {}
        t_slice_done = 0
        t_store_prev = 0
        t_zchunk_prev = 0

        def emit_zeros(zr_in, grp, npk_total, z_scr):
            nonlocal t_zchunk_prev
            nchunk = (npk_total + 85) // 86
            for ch in range(nchunk):
                c0 = 86 * ch
                cw = min(86, npk_total - c0)
                kb.wait("gpsimd", "BG", t_zchunk_prev)
                t_l = dma(kb, "gpsimd", "GD", dq_zp[0:grp, 0:cw], zr_in[:, c0:c0 + cw])
                if kdbg and grp == 86:
                    kb.wait("gpsimd", "GD", t_l)
                    dma(kb, "gpsimd", "GD", dbg_zp[:], dq_zp[0:86, 0:64])
                bg_wait("GD", t_l)
                for jn in range(8):
                    bg(lambda e, jn=jn, g=grp, n=cw: e.tensor_scalar(
                        dq_zi[0:g, 0:8 * n].rearrange("p (c e) -> p c e", e=8)[:, :, jn:jn + 1]
                        .rearrange("p c e -> p (c e)"),
                        dq_zp[0:g, 0:n], 4 * jn, 15,
                        ALU.logical_shift_right, ALU.bitwise_and))
                bg(lambda e: e.tensor_copy(dq_zj[:, 0:86], dq_zp[:, 0:86]))
                bg(lambda e: e.tensor_copy(dq_zj[:, 0:86], dq_zp[:, 0:86]))
                bg(lambda e, g=grp, n=cw: e.tensor_single_scalar(
                    dq_zf[0:g, 0:8 * n], dq_zi[0:g, 0:8 * n], 1.0, ALU.add))
                t_z = bg_tok(lambda e: e.tensor_copy(dq_zj[:, 0:86], dq_zp[:, 0:86]))
                kb.wait("gpsimd", "BG", t_z)
                t_zst = dma(kb, "gpsimd", "GD", z_scr[:, 8 * c0:8 * c0 + 8 * cw],
                            dq_zf[0:grp, 0:8 * cw])
                t_zchunk_prev = t_z
            return t_zst

        def gate_like(qw_in, sc_in, zr_in, z_scr, contrib, agname, out_ag):
            nonlocal t_slice_done, t_store_prev
            t_zst = emit_zeros(zr_in, 4, 1376, z_scr)
            NSL, SLW = 16, 688
            t_st = 0
            for sl in range(NSL):
                c0 = sl * SLW
                kb.wait("gpsimd", "BG", t_slice_done)
                dma(kb, "gpsimd", "GD", dq_qw[0:64, 0:SLW], qw_in[:, c0:c0 + SLW])
                dma(kb, "gpsimd", "GD", dq_qw[64:128, 0:SLW], qw_in[:, c0:c0 + SLW])
                if sl == 0:
                    kb.wait("gpsimd", "GD", t_zst)
                t_sz = None
                for g in range(4):
                    for half in range(2):
                        base = half * 64 + 16 * g
                        dma(kb, "gpsimd", "GD", dq_s[base:base + 16, 0:SLW],
                            sc_in[g:g + 1, c0:c0 + SLW].broadcast_to([16, SLW]))
                        t_sz = dma(kb, "gpsimd", "GD", dq_z[base:base + 16, 0:SLW],
                                   z_scr[g:g + 1, c0:c0 + SLW].broadcast_to([16, SLW]))
                bg_wait("GD", t_sz)
                for mpair in range(4):
                    j = 2 * mpair
                    bg(lambda e, m=mpair: e.tensor_scalar(
                        dq_w4[0:64, 0:688], dq_qw[0:64, 0:688], 8 * m, 15,
                        ALU.logical_shift_right, ALU.bitwise_and))
                    bg(lambda e, m=mpair: e.tensor_scalar(
                        dq_w4[64:128, 0:688], dq_qw[64:128, 0:688], 8 * m + 4, 15,
                        ALU.logical_shift_right, ALU.bitwise_and))
                    bg(lambda e: e.tensor_tensor(dq_f[:, 0:688], dq_w4[:, 0:688],
                                                 dq_z[:, 0:688], ALU.subtract))
                    bg_wait("GD", t_store_prev)
                    t_d = bg_tok(lambda e: e.tensor_tensor(dq_o[:, 0:688], dq_f[:, 0:688],
                                                           dq_s[:, 0:688], ALU.mult))
                    kb.wait("gpsimd", "BG", t_d)
                    dma(kb, "gpsimd", "GD",
                        contrib[:, c0:c0 + SLW].rearrange("(r e) n -> e r n", e=8)[j:j + 1, :, :],
                        dq_o[0:64, 0:688])
                    t_st = dma(kb, "gpsimd", "GD",
                               contrib[:, c0:c0 + SLW].rearrange("(r e) n -> e r n", e=8)[j + 1:j + 2, :, :],
                               dq_o[64:128, 0:688])
                    t_store_prev = t_st
                    t_slice_done = t_d
            kb.wait("gpsimd", "GD", t_st)
            ag_toks[agname] = inc(kb, "gpsimd", "CC",
                                  lambda e, i=contrib, o=out_ag: e.collective_compute(
                                      "AllGather", ALU.bypass, replica_groups=RG,
                                      ins=[i.ap().opt()], outs=[o.ap().opt()]))

        gate_like(gqw, gsc, gzr, zg_scr, g_contrib, "gate", gate_bf)
        gate_like(uqw, usc, uzr, zu_scr, u_contrib, "up", up_bf)

        t_zst = emit_zeros(dzr, 86, 64, zd_scr)
        t_st = 0
        for blk in range(11):
            r0 = 128 * blk
            np_ = min(128, 1376 - r0)
            kb.wait("gpsimd", "BG", t_slice_done)
            dma(kb, "gpsimd", "GD", dq_qw[0:np_, 0:512], dqw[r0:r0 + np_, :])
            if blk == 0:
                kb.wait("gpsimd", "GD", t_zst)
            t_sz = None
            ng = (np_ + 15) // 16
            for g in range(ng):
                base = 16 * g
                cnt = min(16, np_ - base)
                grow = 8 * blk + g
                dma(kb, "gpsimd", "GD", dq_s[base:base + cnt, 0:512],
                    dsc[grow:grow + 1, :].broadcast_to([cnt, 512]))
                t_sz = dma(kb, "gpsimd", "GD", dq_z[base:base + cnt, 0:512],
                           zd_scr[grow:grow + 1, :].broadcast_to([cnt, 512]))
            bg_wait("GD", t_sz)
            for j in range(8):
                bg(lambda e, j=j, n=np_: e.tensor_scalar(
                    dq_w4[0:n, 0:512], dq_qw[0:n, 0:512], 4 * j, 15,
                    ALU.logical_shift_right, ALU.bitwise_and))
                bg(lambda e, n=np_: e.tensor_tensor(dq_f[0:n, 0:512], dq_w4[0:n, 0:512],
                                                    dq_z[0:n, 0:512], ALU.subtract))
                bg_wait("GD", t_store_prev)
                t_d = bg_tok(lambda e, n=np_: e.tensor_tensor(dq_o[0:n, 0:512], dq_f[0:n, 0:512],
                                                              dq_s[0:n, 0:512], ALU.mult))
                kb.wait("gpsimd", "BG", t_d)
                t_st = dma(kb, "gpsimd", "GD",
                           d_contrib.rearrange("(r e) n -> e r n", e=8)[j:j + 1, r0:r0 + np_, :],
                           dq_o[0:np_, 0:512])
                t_store_prev = t_st
                t_slice_done = t_d
                kb.wait("gpsimd", "GD", t_st)
        ag_toks["down"] = inc(kb, "gpsimd", "CC",
                              lambda e: e.collective_compute(
                                  "AllGather", ALU.bypass, replica_groups=RG,
                                  ins=[d_contrib.ap().opt()], outs=[down_bf.ap().opt()]))

        t_st = 0
        for blk in range(4):
            for cs in range(8):
                kb.wait("gpsimd", "BG", t_slice_done)
                t_q = dma(kb, "gpsimd", "GD", dq_qw[:, 0:512],
                          qwo[128 * blk:128 * blk + 128, 512 * cs:512 * cs + 512])
                t_s = dma(kb, "gpsimd", "GD", dq_s[:, 0:512],
                          sclo[blk:blk + 1, 512 * cs:512 * cs + 512].broadcast_to([128, 512]))
                bg_wait("GD", t_s)
                bg(lambda e: e.tensor_single_scalar(dq_f[:, 0:512], dq_qw[:, 0:512],
                                                    8.0, ALU.subtract))
                bg_wait("GD", t_store_prev)
                t_d = bg_tok(lambda e: e.tensor_tensor(dq_o[:, 0:512], dq_f[:, 0:512],
                                                       dq_s[:, 0:512], ALU.mult))
                kb.wait("gpsimd", "BG", t_d)
                t_st = dma(kb, "gpsimd", "GD",
                           o_contrib[128 * blk:128 * blk + 128, 512 * cs:512 * cs + 512],
                           dq_o[:, 0:512])
                t_store_prev = t_st
                t_slice_done = t_d
        kb.wait("gpsimd", "GD", t_st)
        ag_toks["wo"] = inc(kb, "gpsimd", "CC",
                            lambda e: e.collective_compute(
                                "AllGather", ALU.bypass, replica_groups=RG,
                                ins=[o_contrib.ap().opt()], outs=[wo_bf.ap().opt()]))

        for (qw_in, sc_in, w_out, wd_) in ((qwq, sclq, wq_bf, 512),
                                           (qwk, sclk, wk_bf, 128),
                                           (qwv, sclv, wv_bf, 128)):
            for it in range(NIT):
                kb.wait("gpsimd", "BG", t_slice_done)
                t_l = dma(kb, "gpsimd", "GD", dq_qw[:, 0:wd_], qw_in[128 * it:128 * it + 128, :])
                t_s = dma(kb, "gpsimd", "GD", dq_s[:, 0:wd_],
                          sc_in[it:it + 1, :].broadcast_to([128, wd_]))
                bg_wait("GD", t_s)
                bg(lambda e, w=wd_: e.tensor_single_scalar(dq_f[:, 0:w], dq_qw[:, 0:w],
                                                           8.0, ALU.subtract))
                bg_wait("GD", t_store_prev)
                t_d = bg_tok(lambda e, w=wd_: e.tensor_tensor(dq_o[:, 0:w], dq_f[:, 0:w],
                                                              dq_s[:, 0:w], ALU.mult))
                kb.wait("gpsimd", "BG", t_d)
                t_store_prev = dma(kb, "gpsimd", "GD", w_out[128 * it:128 * it + 128, :],
                                   dq_o[:, 0:wd_])
                t_slice_done = t_d
        t_qkvw_done = kb.cnt["GD"]

        if nphase >= 1:
            with ExitStack() as p1:
                xn1 = kb.sb(p1, "xn1", [128, NIT * TOK], BF16)
                with ExitStack() as lns:
                    emit_ln(lns, xin, 0, xn1, 0)
                kb.pump_bg(120)
                with ExitStack() as lns:
                    t_xn1 = emit_ln(lns, xin, 1, xn1, NIT * TC)
                kb.pump_bg(120)
                kb.wait("sync", "VE", t_xn1)
                kb.wait("sync", "SE", ln_prev["sq"])
                kb.wait("sync", "TE", ln_prev["ssq"])
                wrow_f = kb.sb(p1, "wrow_f", [128, 2048], F32)
                wrow_b = kb.sb(p1, "wrow_b", [128, D], BF16)
                wT = kb.sb(p1, "wT", [128, NIT * 128], BF16)
                xrl = [kb.sb(p1, f"xrl{i}", [128, TC], F32) for i in range(2)]
                ost = [kb.sb(p1, f"ost{i}", [128, TC], F32) for i in range(2)]
                if kdbg:
                    kb.wait("gpsimd", "VE", t_xn1)
                    dma(kb, "gpsimd", "GD", dbg_xn[:], xn1[:])
                kb.wait("tensor", "VE", t_xn1)
                t_cast_last = 0
                t_tr_done = 0
                t_ep = {}
                t_ost_st = {}
                for oc in range(NIT):
                    for half in range(2):
                        kb.wait("sync", "SE", t_cast_last)
                        t_wl = dma(kb, "sync", "LW0", wrow_f[:],
                                   aw[128 * oc:128 * oc + 128, 2048 * half:2048 * half + 2048])
                        kb.wait("scalar", "LW0", t_wl)
                        kb.wait("scalar", "TE", t_tr_done)
                        t_cast_last = inc(kb, "scalar", "SE",
                                          lambda e, h=half: e.activation(
                                              wrow_b[:, 2048 * h:2048 * h + 2048], wrow_f[:], AF.Copy))
                    kb.wait("tensor", "SE", t_cast_last)
                    t_ev = 0
                    for g4 in range(8):
                        kb.wait("tensor", "VE", t_ev)
                        t_tp = None
                        for k4 in range(4):
                            it = 4 * g4 + k4
                            t_tp = inc(kb, "tensor", "TE",
                                       lambda e, it=it, k4=k4: e.matmul(
                                           pb[2][:, 128 * k4:128 * k4 + 128],
                                           wrow_b[:, 128 * it:128 * it + 128], ident[:],
                                           start=True, stop=True))
                        kb.wait("vector", "TE", t_tp)
                        t_ev = inc(kb, "vector", "VE",
                                   lambda e, g4=g4: e.tensor_copy(
                                       wT[:, 512 * g4:512 * g4 + 512], pb[2][:, :]))
                    t_tr_done = kb.cnt["TE"]
                    if kdbg and oc == 0:
                        kb.wait("gpsimd", "VE", t_ev)
                        dma(kb, "gpsimd", "GD", dbg_wt[:], wT[:])
                    kb.wait("tensor", "VE", t_ev)
                    for tcc in range(2):
                        kb.wait("sync", "VE", t_ep.get((oc - 1, tcc), 0))
                        t_xr = dma(kb, "sync", f"LX{tcc}", xrl[tcc][:],
                                   xin[128 * oc:128 * oc + 128, TC * tcc:TC * tcc + TC])
                        pbank = pb[tcc]
                        kb.wait("tensor", "VE", t_ep.get((oc - 1, tcc), 0))
                        t_mm = None
                        for it in range(NIT):
                            t_mm = inc(kb, "tensor", "TE",
                                       lambda e, it=it, tcc=tcc, pbank=pbank: e.matmul(
                                           pbank[:, :], wT[:, 128 * it:128 * it + 128],
                                           xn1[:, NIT * TC * tcc + TC * it:NIT * TC * tcc + TC * it + TC],
                                           start=(it == 0), stop=(it == NIT - 1)))
                        kb.wait("vector", "TE", t_mm)
                        kb.wait("vector", f"LX{tcc}", t_xr)
                        kb.wait("vector", f"SD{tcc}", t_ost_st.get((oc - 1, tcc), 0))
                        t_ep[(oc, tcc)] = inc(kb, "vector", "VE",
                                              lambda e, oc=oc, tcc=tcc, pbank=pbank: e.scalar_tensor_tensor(
                                                  ost[tcc][:], pbank[:, :], biast[:, oc:oc + 1],
                                                  xrl[tcc][:], ALU.add, ALU.add))
                        kb.wait("scalar", "VE", t_ep[(oc, tcc)])
                        t_ost_st[(oc, tcc)] = dma(
                            kb, "scalar", f"SD{tcc}",
                            x_dram[128 * oc:128 * oc + 128, TC * tcc:TC * tcc + TC], ost[tcc][:])
                    kb.pump_bg(34)

        def emit_ffn(src_dram, dst_dram, tag):
            t_dst_last = 0
            with ExitStack() as pf:
                xnf = kb.sb(pf, "xnf" + tag, [128, NIT * TC], BF16)
                t_gu_last = 0
                t_down_last = 0
                t_dep_last = 0
                t_od_last = [0, 0]
                for tcc in range(2):
                    kb.wait("sync", "TE", t_gu_last)
                    kb.wait("sync", "VE", t_dep_last)
                    kb.wait("sync", "SD0", t_od_last[0])
                    kb.wait("sync", "SD1", t_od_last[1])
                    with ExitStack() as lns:
                        t_xn = emit_ln(lns, src_dram, tcc, xnf, 0, vec_war=t_gu_last)
                    kb.pump_bg(90)
                    kb.wait("sync", "VE", t_xn)
                    kb.wait("sync", "SE", ln_prev["sq"])
                    kb.wait("sync", "TE", ln_prev["ssq"])
                    with ExitStack() as gus:
                        hbuf = kb.sb(gus, "h" + tag, [128, NI * TC], BF16)
                        gup = ExitStack()
                        wg = [kb.sb(gup, f"wg{i}" + tag, [128, NIT * 128], BF16) for i in range(2)]
                        wu = [kb.sb(gup, f"wu{i}" + tag, [128, NIT * 128], BF16) for i in range(2)]
                        sil = kb.sb(gup, "sil" + tag, [128, TC], BF16)
                        kb.wait("tensor", "VE", t_xn)
                        t_wg_used = {}
                        t_wu_used = {}
                        t_h = {}
                        t_sil_t = {}
                        for nci in range(NI):
                            pbuf = nci % 2
                            if nci == 0 and tcc == 0:
                                kb.wait("sync", "CC", ag_toks["up"])
                            kb.wait("sync", "TE", t_wg_used.get(nci - 2, 0))
                            t_gl = dma(kb, "sync", f"LW{pbuf}",
                                       wg[pbuf][:].rearrange("p (k n) -> p k n", n=128),
                                       gate_bf[:, 128 * nci:128 * nci + 128]
                                       .rearrange("(k p) n -> p k n", p=128))
                            kb.wait("sync", "TE", t_wu_used.get(nci - 2, 0))
                            t_ul = dma(kb, "sync", f"LU{pbuf}",
                                       wu[pbuf][:].rearrange("p (k n) -> p k n", n=128),
                                       up_bf[:, 128 * nci:128 * nci + 128]
                                       .rearrange("(k p) n -> p k n", p=128))
                            if nci == 0 and tcc == 0:
                                kb.wait("tensor", "CC", ag_toks["gate"])
                                kb.wait("tensor", "CC", ag_toks["up"])
                            kb.wait("tensor", f"LW{pbuf}", t_gl)
                            pg = pb[0] if pbuf == 0 else pb[1]
                            pu = pb[2] if pbuf == 0 else pb[3]
                            kb.wait("tensor", "SE", t_sil_t.get(nci - 2, 0))
                            kb.wait("tensor", "VE", t_h.get(nci - 2, 0))
                            t_gmm = None
                            for it in range(NIT):
                                t_gmm = inc(kb, "tensor", "TE",
                                            lambda e, it=it, pbuf=pbuf, pg=pg, wg=wg: e.matmul(
                                                pg[:, :], wg[pbuf][:, 128 * it:128 * it + 128],
                                                xnf[:, TC * it:TC * it + TC],
                                                start=(it == 0), stop=(it == NIT - 1)))
                            t_wg_used[nci] = t_gmm
                            kb.wait("tensor", f"LU{pbuf}", t_ul)
                            t_umm = None
                            for it in range(NIT):
                                t_umm = inc(kb, "tensor", "TE",
                                            lambda e, it=it, pbuf=pbuf, pu=pu, wu=wu: e.matmul(
                                                pu[:, :], wu[pbuf][:, 128 * it:128 * it + 128],
                                                xnf[:, TC * it:TC * it + TC],
                                                start=(it == 0), stop=(it == NIT - 1)))
                            t_wu_used[nci] = t_umm
                            kb.wait("scalar", "TE", t_gmm)
                            kb.wait("scalar", "VE", t_h.get(nci - 1, 0))
                            t_sil_t[nci] = inc(kb, "scalar", "SE",
                                               lambda e, pg=pg, sil=sil: e.activation(sil[:], pg[:, :], AF.Silu))
                            kb.wait("vector", "SE", t_sil_t[nci])
                            kb.wait("vector", "TE", t_umm)
                            if nci == 0:
                                kb.wait("vector", "TE", t_down_last)
                            t_h[nci] = inc(kb, "vector", "VE",
                                           lambda e, nci=nci, pu=pu, hbuf=hbuf, sil=sil: e.tensor_tensor(
                                               hbuf[:, TC * nci:TC * nci + TC], sil[:], pu[:, :],
                                               ALU.mult))
                            kb.pump_bg(4)
                        t_h_all = t_h[NI - 1]
                        kb.pump_bg(400)
                        gup.close()
                        kb.wait("sync", "TE", t_wu_used[NI - 1])
                        kb.wait("sync", "VE", t_h_all)
                        wd = [kb.sb(gus, f"wd{i}" + tag, [128, NI * 128], BF16) for i in range(2)]
                        xr2 = [kb.sb(gus, f"xr2{i}" + tag, [128, TC], F32) for i in range(2)]
                        od = [kb.sb(gus, f"od{i}" + tag, [128, TC], F32) for i in range(2)]
                        t_wd_used = {}
                        t_dep = {}
                        t_od_st = {}
                        for oci in range(NIT):
                            pbuf = oci % 2
                            rk = oci // 4
                            if oci == 0 and tcc == 0:
                                kb.wait("sync", "CC", ag_toks["down"])
                            kb.wait("sync", "TE", t_wd_used.get(oci - 2, 0))
                            t_dl = dma(kb, "sync", f"LDN{pbuf}",
                                       wd[pbuf][:].rearrange("p (k n) -> p k n", n=128),
                                       down_bf[INTER * rk:INTER * rk + INTER,
                                               128 * (oci % 4):128 * (oci % 4) + 128]
                                       .rearrange("(k p) n -> p k n", p=128))
                            kb.wait("sync", "VE", t_dep.get(oci - 2, 0))
                            t_xr = dma(kb, "sync", f"LX{pbuf}", xr2[pbuf][:],
                                       src_dram[128 * oci:128 * oci + 128, TC * tcc:TC * tcc + TC])
                            if oci == 0 and tcc == 0:
                                kb.wait("tensor", "CC", ag_toks["down"])
                            if oci == 0:
                                kb.wait("tensor", "VE", t_h_all)
                            kb.wait("tensor", f"LDN{pbuf}", t_dl)
                            pd = pb[4] if pbuf == 0 else pb[5]
                            kb.wait("tensor", "VE", t_dep.get(oci - 2, 0))
                            t_dmm = None
                            for kt in range(NI):
                                t_dmm = inc(kb, "tensor", "TE",
                                            lambda e, kt=kt, pbuf=pbuf, pd=pd, wd=wd, hbuf=hbuf: e.matmul(
                                                pd[:, :], wd[pbuf][:, 128 * kt:128 * kt + 128],
                                                hbuf[:, TC * kt:TC * kt + TC],
                                                start=(kt == 0), stop=(kt == NI - 1)))
                            t_wd_used[oci] = t_dmm
                            kb.wait("vector", "TE", t_dmm)
                            kb.wait("vector", f"LX{pbuf}", t_xr)
                            kb.wait("vector", f"SD{pbuf}", t_od_st.get((tcc, oci - 2), 0))
                            t_dep[oci] = inc(kb, "vector", "VE",
                                             lambda e, pbuf=pbuf, pd=pd, od=od, xr2=xr2: e.tensor_add(
                                                 od[pbuf][:], xr2[pbuf][:], pd[:, :]))
                            kb.wait("scalar", "VE", t_dep[oci])
                            t_od_st[(tcc, oci)] = dma(kb, "scalar", f"SD{pbuf}",
                                                      dst_dram[128 * oci:128 * oci + 128,
                                                               TC * tcc:TC * tcc + TC],
                                                      od[pbuf][:])
                            t_dst_last = t_od_st[(tcc, oci)]
                            kb.pump_bg(8)
                        t_down_last = t_wd_used[NIT - 1]
                        t_gu_last = kb.cnt["TE"]
                        t_dep_last = t_dep[NIT - 1]
                        t_od_last = [t_od_st[(tcc, NIT - 2)], t_od_st[(tcc, NIT - 1)]]
                        if (NIT - 1) % 2 == 0:
                            t_od_last = [t_od_st[(tcc, NIT - 1)], t_od_st[(tcc, NIT - 2)]]
            return t_dst_last

        if nphase >= 2:
            kb.barrier(exclude=("CC", "GD", "BG"), flush_bg=False)
            emit_ffn(x_dram, x_dram, "f1")
            kb.pump_bg(10 ** 9)
            if kdbg:
                kb.wait("gpsimd", "CC", ag_toks["down"])
                dma(kb, "gpsimd", "GD", dbg_g[:], down_bf[:])
                dma(kb, "gpsimd", "GD", dbg_z[:], zd_scr[:])
                dma(kb, "gpsimd", "GD", dbg_zg[:], zg_scr[:])

        kb.barrier()
        p0.close()

        if nphase >= 2.25:
            with ExitStack() as p3a:
                xn3 = kb.sb(p3a, "xn3", [128, NIT * TC], BF16)
                t_prev_st = 0
                for tcc in range(2):
                    kb.wait("vector", "GD", t_prev_st)
                    with ExitStack() as lns:
                        t_xn = emit_ln(lns, x_dram, tcc, xn3, 0)
                    kb.wait("gpsimd", "VE", t_xn)
                    t_prev_st = dma(kb, "gpsimd", "GD",
                                    xn_contrib[tcc][:, :]
                                    .rearrange("(i p) t -> p i t", p=128),
                                    xn3[:].rearrange("p (i t) -> p i t", t=TC))
                    kb.wait("gpsimd", "GD", t_prev_st)
                    ag_toks[f"xn{tcc}"] = inc(kb, "gpsimd", "CC",
                                              lambda e, tcc=tcc: e.collective_compute(
                                                  "AllGather", ALU.bypass, replica_groups=RG,
                                                  ins=[xn_contrib[tcc].ap().opt()],
                                                  outs=[xn_ag[tcc].ap().opt()]))

            kb.barrier()
        if nphase >= 2.5:
            with ExitStack() as p3b:
                cosq = kb.sb(p3b, "cosq", [128, S], F32)
                sinq = kb.sb(p3b, "sinq", [128, S], F32)
                cosk = kb.sb(p3b, "cosk", [128, S], F32)
                sink = kb.sb(p3b, "sink", [128, S], F32)
                t_tbl = 0
                for dst, src in ((cosq, c_cosq), (sinq, c_sinq), (cosk, c_cosk), (sink, c_sink)):
                    t_tbl = dma(kb, "sync", "LD", dst[:], src[:])
                xsec = [kb.sb(p3b, f"xsec{i}", [128, 16 * 1024], BF16) for i in range(2)]
                wpb = [kb.sb(p3b, f"wpb{i}", [128, NIT * 128], BF16) for i in range(2)]
                ptmp = kb.sb(p3b, "ptmp", [128, 1024], BF16)
                rtmp1 = kb.sb(p3b, "rtmp1", [128, 1024], F32)
                rtmp2 = kb.sb(p3b, "rtmp2", [128, 1024], F32)
                rout = kb.sb(p3b, "rout", [128, 1024], BF16)
                kb.wait("sync", "CC", ag_toks["xn1"])
                kb.wait("sync", "GD", t_qkvw_done)
                kb.wait("tensor", "LD", t_tbl)
                t_xs_used = {}
                t_xs_ld = {}
                t_w_used = {}
                t_ev = {}
                t_ptmp_ve = 0
                t_ptmp_gd = 0
                t_pb45_ve = 0
                t_rout_gd = 0
                for sec in range(8):
                    bs, qb = sec // 4, sec % 4
                    for hf in range(2):
                        kb.wait("sync", "TE", t_xs_used.get((sec - 1, hf), 0))
                        dma(kb, "sync", f"LX{hf}",
                            xsec[hf][:].rearrange("p (i t) -> p (i t)" if False else "p (i t) -> p i t", t=1024)[:, :, 0:TC],
                            xn_ag[0][D * sec + 2048 * hf: D * sec + 2048 * hf + 2048, :]
                            .rearrange("(i p) t -> p i t", p=128))
                        t_xs_ld[(sec, hf)] = dma(
                            kb, "sync", f"LX{hf}",
                            xsec[hf][:].rearrange("p (i t) -> p i t", t=1024)[:, :, TC:1024],
                            xn_ag[1][D * sec + 2048 * hf: D * sec + 2048 * hf + 2048, :]
                            .rearrange("(i p) t -> p i t", p=128))
                    for ock in range(6):
                        wbuf = wpb[ock % 2]
                        wpar = ock % 2
                        kb.wait("sync", "TE", t_w_used.get(6 * sec + ock - 2, 0))
                        if ock < 4:
                            src_w = wq_bf[:, 128 * ock:128 * ock + 128]
                        elif ock == 4:
                            src_w = wk_bf[:, :]
                        else:
                            src_w = wv_bf[:, :]
                        t_wl = dma(kb, "sync", f"LW{wpar}",
                                   wbuf[:].rearrange("p (k n) -> p k n", n=128),
                                   src_w.rearrange("(k p) n -> p k n", p=128))
                        kb.wait("tensor", f"LW{wpar}", t_wl)
                        t_mm_last = None
                        for tq in range(2):
                            pbank = pb[2 * (ock % 2) + tq]
                            kb.wait("tensor", "SE", t_ev.get(2 * (ock % 2) + tq, 0))
                            for hf in range(2):
                                kb.wait("tensor", f"LX{hf}", t_xs_ld[(sec, hf)])
                                for itl in range(16):
                                    first = (hf == 0 and itl == 0)
                                    last = (hf == 1 and itl == 15)
                                    t_mm_last = inc(kb, "tensor", "TE",
                                                    lambda e, hf=hf, itl=itl, tq=tq,
                                                    pbank=pbank, wbuf=wbuf,
                                                    first=first, last=last: e.matmul(
                                                        pbank[:, :],
                                                        wbuf[:, 128 * (16 * hf + itl):128 * (16 * hf + itl) + 128],
                                                        xsec[hf][:, 1024 * itl + 512 * tq:1024 * itl + 512 * tq + 512],
                                                        start=first, stop=last))
                        t_w_used[6 * sec + ock] = t_mm_last
                        t_xs_used[(sec, 0)] = t_mm_last
                        t_xs_used[(sec, 1)] = t_mm_last
                        for tq in range(2):
                            pbank = pb[2 * (ock % 2) + tq]
                            kb.wait("scalar", "TE", t_mm_last)
                            kb.wait("scalar", "VE", t_ptmp_ve)
                            kb.wait("scalar", "GD", t_ptmp_gd)
                            t_ev[2 * (ock % 2) + tq] = inc(
                                kb, "scalar", "SE",
                                lambda e, tq=tq, pbank=pbank: e.activation(
                                    ptmp[:, 512 * tq:512 * tq + 512], pbank[:, :], AF.Copy))
                        t_evd = max(t_ev[2 * (ock % 2)], t_ev[2 * (ock % 2) + 1])
                        if ock == 5:
                            kb.wait("gpsimd", "SE", t_evd)
                            t_ptmp_gd = dma(kb, "gpsimd", "GD",
                                            v_scr[128 * bs:128 * bs + 128,
                                                  1024 * qb:1024 * qb + 1024], ptmp[:])
                        else:
                            cost = cosq if ock < 4 else cosk
                            sint = sinq if ock < 4 else sink
                            kb.wait("tensor", "SE", t_evd)
                            kb.wait("tensor", "VE", t_pb45_ve)
                            t_rt = None
                            for tq in range(2):
                                t_rt = inc(kb, "tensor", "TE",
                                           lambda e, tq=tq: e.matmul(
                                               (pb[4] if tq == 0 else pb[5])[:, :],
                                               rott[:], ptmp[:, 512 * tq:512 * tq + 512],
                                               start=True, stop=True))
                            kb.wait("vector", "SE", t_evd)
                            op(kb, "vector", lambda e, qb=qb, cost=cost: e.tensor_tensor(
                                rtmp1[:], ptmp[:], cost[:, 1024 * qb:1024 * qb + 1024], ALU.mult))
                            t_ptmp_ve = kb.cnt["VE"] + 1
                            kb.wait("vector", "TE", t_rt)
                            op(kb, "vector", lambda e, qb=qb, sint=sint: e.tensor_tensor(
                                rtmp2[:, 0:512], pb[4][:, :],
                                sint[:, 1024 * qb:1024 * qb + 512], ALU.mult))
                            op(kb, "vector", lambda e, qb=qb, sint=sint: e.tensor_tensor(
                                rtmp2[:, 512:1024], pb[5][:, :],
                                sint[:, 1024 * qb + 512:1024 * qb + 1024], ALU.mult))
                            kb.wait("vector", "GD", t_rout_gd)
                            t_rp = inc(kb, "vector", "VE",
                                       lambda e: e.tensor_add(rout[:], rtmp1[:], rtmp2[:]))
                            t_pb45_ve = t_rp
                            assert t_ptmp_ve == t_rp
                            kb.wait("gpsimd", "VE", t_rp)
                            if ock < 4:
                                dstap = q_scr[128 * (4 * bs + ock):128 * (4 * bs + ock) + 128,
                                              1024 * qb:1024 * qb + 1024]
                            else:
                                dstap = k_scr[128 * bs:128 * bs + 128,
                                              1024 * qb:1024 * qb + 1024]
                            t_rout_gd = dma(kb, "gpsimd", "GD", dstap, rout[:])
                t_qkv_st_all = kb.cnt["GD"]

            kb.barrier()
        if nphase >= 2.75:
            with ExitStack() as p3c:
                qsb = [kb.sb(p3c, f"qsb{i}", [128, S], BF16) for i in range(2)]
                ksb = [kb.sb(p3c, f"ksb{i}", [128, S], BF16) for i in range(2)]
                vtb = [kb.sb(p3c, f"vtb{i}", [128, S], BF16) for i in range(2)]
                vnat = [kb.sb(p3c, f"vnat{i}", [128, NIT * 128], BF16) for i in range(2)]
                srow = kb.sb(p3c, "srow", [128, S], F32)
                atn = kb.sb(p3c, "atn", [128, S], BF16)
                atT = [kb.sb(p3c, f"atT{i}", [128, 512], BF16) for i in range(2)]
                ctxev = kb.sb(p3c, "ctxev", [128, 512], BF16)
                kb.wait("sync", "GD", t_qkv_st_all)
                t_b_used = {}
                t_vn = {}
                t_sc_ev = {}
                t_exp_prev = 0
                t_tr_last_prev = 0
                t_at_used = {}
                t_ctx_st = 0
                t_ctx_ev_tok = 0
                for bh in range(8):
                    b_, h_ = bh // 4, bh % 4
                    par = bh % 2
                    kb.wait("sync", "TE", t_b_used.get(bh - 2, 0))
                    t_ql = dma(kb, "sync", f"LA{par}", qsb[par][:],
                               q_scr[128 * (4 * b_ + h_):128 * (4 * b_ + h_) + 128, :])
                    t_kl = dma(kb, "sync", f"LA{par}", ksb[par][:],
                               k_scr[128 * b_:128 * b_ + 128, :])
                    t_vl = dma(kb, "sync", f"LA{par}", vtb[par][:],
                               v_scr[128 * b_:128 * b_ + 128, :])
                    kb.wait("tensor", f"LA{par}", t_vl)
                    t_ev_v = t_vn.get(bh - 2, 0)
                    for g4 in range(8):
                        kb.wait("tensor", "VE", t_ev_v)
                        t_tp = None
                        for k4 in range(4):
                            it = 4 * g4 + k4
                            t_tp = inc(kb, "tensor", "TE",
                                       lambda e, it=it, k4=k4, par=par: e.matmul(
                                           pb[2][:, 128 * k4:128 * k4 + 128],
                                           vtb[par][:, 128 * it:128 * it + 128], ident[:],
                                           start=True, stop=True))
                        kb.wait("vector", "TE", t_tp)
                        t_ev_v = inc(kb, "vector", "VE",
                                     lambda e, g4=g4, par=par: e.tensor_copy(
                                         vnat[par][:, 512 * g4:512 * g4 + 512], pb[2][:, :]))
                    t_vn[bh] = t_ev_v
                    kb.wait("tensor", f"LA{par}", max(t_ql, t_kl))
                    for t in range(32):
                        KL = 128 * (t + 1)
                        nch = (KL + 511) // 512
                        t_sc_mm = {}
                        t_evs = []
                        for ch in range(nch):
                            w = min(512, KL - 512 * ch)
                            pbank = pb[ch % 2]
                            kb.wait("tensor", "VE", t_sc_ev.get(("v", ch % 2), 0))
                            kb.wait("tensor", "SE", t_sc_ev.get(("s", ch % 2), 0))
                            t_sc_mm[ch] = inc(kb, "tensor", "TE",
                                              lambda e, t=t, ch=ch, w=w, par=par, pbank=pbank: e.matmul(
                                                  pbank[:, 0:w],
                                                  qsb[par][:, 128 * t:128 * t + 128],
                                                  ksb[par][:, 512 * ch:512 * ch + w],
                                                  start=True, stop=True))
                            last = (ch == nch - 1)
                            if last:
                                if w > 128:
                                    kb.wait("scalar", "TE", t_sc_mm[ch])
                                    kb.wait("scalar", "SE", t_exp_prev)
                                    kb.wait("scalar", "VE", t_sc_ev.get(("v", ch % 2), 0))
                                    t_e1 = inc(kb, "scalar", "SE",
                                               lambda e, ch=ch, w=w, pbank=pbank: e.copy(
                                                   srow[:, 512 * ch:512 * ch + w - 128],
                                                   pbank[:, 0:w - 128]))
                                    t_sc_ev[("s", ch % 2)] = t_e1
                                    t_evs.append(("SE", t_e1))
                                kb.wait("vector", "TE", t_sc_mm[ch])
                                kb.wait("vector", "SE", t_exp_prev)
                                kb.wait("vector", "SE", t_sc_ev.get(("s", ch % 2), 0))
                                t_e2 = inc(kb, "vector", "VE",
                                           lambda e, ch=ch, w=w, pbank=pbank: e.tensor_add(
                                               srow[:, 512 * ch + w - 128:512 * ch + w],
                                               pbank[:, w - 128:w], maskt[:]))
                                t_sc_ev[("v", ch % 2)] = t_e2
                                t_evs.append(("VE", t_e2))
                            else:
                                kb.wait("scalar", "TE", t_sc_mm[ch])
                                kb.wait("scalar", "SE", t_exp_prev)
                                kb.wait("scalar", "VE", t_sc_ev.get(("v", ch % 2), 0))
                                t_e1 = inc(kb, "scalar", "SE",
                                           lambda e, ch=ch, pbank=pbank: e.copy(
                                               srow[:, 512 * ch:512 * ch + 512], pbank[:, :]))
                                t_sc_ev[("s", ch % 2)] = t_e1
                                t_evs.append(("SE", t_e1))
                        for sname, tv in t_evs:
                            kb.wait("vector", sname, tv)
                        t_max = inc(kb, "vector", "VE",
                                    lambda e, KL=KL: e.reduce_max(nmax[:], srow[:, 0:KL],
                                                                  axis=AX.X, negate=True))
                        kb.wait("scalar", "VE", t_max)
                        kb.wait("scalar", "TE", t_tr_last_prev)
                        t_exp = inc(kb, "scalar", "SE",
                                    lambda e, KL=KL: e.activation(
                                        atn[:, 0:KL], srow[:, 0:KL], AF.Exp,
                                        bias=nmax[:, 0:1], accum_out=rsum[:]))
                        t_exp_prev = t_exp
                        kb.wait("vector", "SE", t_exp)
                        op(kb, "vector", lambda e: e.reciprocal(rinv[:], rsum[:]))
                        kb.wait("vector", "TE", t_tr_last_prev)
                        t_dm = inc(kb, "vector", "VE",
                                   lambda e: e.tensor_scalar_mul(dmt[:], ident[:], rinv[:, 0:1]))
                        pc = pb[4]
                        kb.wait("tensor", "VE", t_dm)
                        nkt = t + 1
                        ngrp = (nkt + 3) // 4
                        grp_ev = {}

                        def emit_trans(kg):
                            tpar = kg % 2
                            ptr = pb[2] if tpar == 0 else pb[3]
                            n_in_g = min(4, nkt - 4 * kg)
                            kb.wait("tensor", "VE", t_at_used.get(("ev", tpar), 0))
                            t_tp = None
                            for k4 in range(n_in_g):
                                kt = 4 * kg + k4
                                t_tp = inc(kb, "tensor", "TE",
                                           lambda e, kt=kt, k4=k4, ptr=ptr: e.matmul(
                                               ptr[:, 128 * k4:128 * k4 + 128],
                                               atn[:, 128 * kt:128 * kt + 128], dmt[:],
                                               start=True, stop=True))
                            kb.wait("vector", "TE", t_tp)
                            kb.wait("vector", "TE", t_at_used.get(("mm", tpar), 0))
                            grp_ev[kg] = inc(kb, "vector", "VE",
                                             lambda e, tpar=tpar, n=n_in_g: e.tensor_copy(
                                                 atT[tpar][:, 0:128 * n],
                                                 (pb[2] if tpar == 0 else pb[3])[:, 0:128 * n]))
                            t_at_used[("ev", tpar)] = grp_ev[kg]

                        def emit_ctx(kg):
                            tpar = kg % 2
                            n_in_g = min(4, nkt - 4 * kg)
                            kb.wait("tensor", "VE", grp_ev[kg])
                            if kg == 0:
                                kb.wait("tensor", "SE", t_ctx_ev_tok)
                            t_cmm = None
                            for k4 in range(n_in_g):
                                kt = 4 * kg + k4
                                t_cmm = inc(kb, "tensor", "TE",
                                            lambda e, kt=kt, k4=k4, par=par, t=t, tpar=tpar,
                                            first=(kt == 0), last=(kt == t): e.matmul(
                                                pc[:, 128 * (t % 4):128 * (t % 4) + 128],
                                                vnat[par][:, 128 * kt:128 * kt + 128],
                                                atT[tpar][:, 128 * k4:128 * k4 + 128],
                                                start=first, stop=last))
                            t_at_used[("mm", tpar)] = t_cmm

                        for kg in range(ngrp):
                            emit_trans(kg)
                            if kg > 0:
                                emit_ctx(kg - 1)
                        emit_ctx(ngrp - 1)
                        t_tr_last_prev = kb.cnt["TE"]
                        if t % 4 == 3:
                            kb.wait("scalar", "TE", t_tr_last_prev)
                            kb.wait("scalar", "GD", t_ctx_st)
                            t_ctx_ev_tok = inc(kb, "scalar", "SE",
                                               lambda e: e.copy(ctxev[:], pc[:, :]))
                            kb.wait("gpsimd", "SE", t_ctx_ev_tok)
                            t_ctx_st = dma(kb, "gpsimd", "GD",
                                           ctx_contrib[512 * b_ + 128 * h_:512 * b_ + 128 * h_ + 128,
                                                       512 * (t // 4):512 * (t // 4) + 512],
                                           ctxev[:])
                    t_b_used[bh] = kb.cnt["TE"]
                kb.wait("gpsimd", "GD", t_ctx_st)
                ag_toks["ctx"] = inc(kb, "gpsimd", "CC",
                                     lambda e: e.collective_compute(
                                         "AllGather", ALU.bypass, replica_groups=RG,
                                         ins=[ctx_contrib.ap().opt()], outs=[ctx_ag.ap().opt()]))

            kb.barrier()
        if nphase >= 3:
            with ExitStack() as p3d:
                ctxT = kb.sb(p3d, "ctxT", [128, NIT * 1024], BF16)
                wob = [kb.sb(p3d, f"wob{i}", [128, NIT * 128], BF16) for i in range(2)]
                xr3 = [kb.sb(p3d, f"xr3{i}", [128, TC], F32) for i in range(2)]
                od3 = [kb.sb(p3d, f"od3{i}", [128, TC], F32) for i in range(2)]
                cell = {}
                kb.wait("gpsimd", "LD", t_const)
                op(kb, "gpsimd", lambda e: cell.__setitem__("bv", nc.values_load(
                    ridb[0:1, 0:1], engines=[ET.Pool], min_val=0, max_val=1,
                    skip_runtime_bounds_check=True)))
                op(kb, "gpsimd", lambda e: cell.__setitem__("qv", nc.values_load(
                    ridq[0:1, 0:1], engines=[ET.Pool], min_val=0, max_val=3,
                    skip_runtime_bounds_check=True)))
                kb.wait("gpsimd", "CC", ag_toks["ctx"])
                view2 = ctx_ag.rearrange("(blk two hh p) (qc t) -> p blk hh two qc t",
                                         two=2, hh=4, p=128, qc=4)
                h_gd = kb.sems["GD"]
                t_ctl = 0
                for blk in range(8):
                    t_ctl = kb.tok("GD", 16)
                    out_ap = ctxT[:].rearrange("p (blk hh t) -> p blk hh t",
                                               hh=4, t=1024)[:, blk:blk + 1, :, :]

                    def dyn_load(e, blk=blk, h_gd=h_gd, out_ap=out_ap):
                        src = view2[:, blk:blk + 1, :, bass.ts(cell["bv"], 1),
                                    bass.ts(cell["qv"], 1), :]
                        return e.dma_start(out=out_ap, in_=src).then_inc(h_gd, 16)
                    kb.emit("gpsimd", dyn_load)
                if kdbg:
                    kb.wait("gpsimd", "GD", t_ctl)
                    dma(kb, "gpsimd", "GD", dbg_ct[:], ctxT[:])
                kb.wait("tensor", "GD", t_ctl)
                kb.wait("tensor", "CC", ag_toks["wo"])
                t_w_used3 = {}
                t_ep3 = {}
                t_ost3 = {}
                for oci in range(NIT):
                    pbuf = oci % 2
                    if oci == 0:
                        kb.wait("sync", "CC", ag_toks["wo"])
                    kb.wait("sync", "TE", t_w_used3.get(oci - 2, 0))
                    t_wl = dma(kb, "sync", f"LW{pbuf}",
                               wob[pbuf][:].rearrange("p (k n) -> p k n", n=128),
                               wo_bf[:, 128 * oci:128 * oci + 128]
                               .rearrange("(k p) n -> p k n", p=128))
                    kb.wait("tensor", f"LW{pbuf}", t_wl)
                    t_mm2 = {}
                    for tcc in range(2):
                        pbank = pb[tcc]
                        kb.wait("tensor", "VE", t_ep3.get((oci - 1, tcc), 0))
                        t_mm = None
                        for it in range(NIT):
                            t_mm = inc(kb, "tensor", "TE",
                                       lambda e, it=it, tcc=tcc, pbuf=pbuf, pbank=pbank: e.matmul(
                                           pbank[:, :], wob[pbuf][:, 128 * it:128 * it + 128],
                                           ctxT[:, 1024 * it + 512 * tcc:1024 * it + 512 * tcc + 512],
                                           start=(it == 0), stop=(it == NIT - 1)))
                        t_mm2[tcc] = t_mm
                    t_w_used3[oci] = t_mm2[1]
                    for tcc in range(2):
                        pbank = pb[tcc]
                        kb.wait("sync", "VE", t_ep3.get((oci - 1, tcc), 0))
                        t_xr = dma(kb, "sync", f"LX{tcc}", xr3[tcc][:],
                                   x_dram[128 * oci:128 * oci + 128, TC * tcc:TC * tcc + TC])
                        kb.wait("vector", "TE", t_mm2[tcc])
                        kb.wait("vector", f"LX{tcc}", t_xr)
                        kb.wait("vector", f"SD{tcc}", t_ost3.get((oci - 1, tcc), 0))
                        t_ep3[(oci, tcc)] = inc(kb, "vector", "VE",
                                                lambda e, tcc=tcc, pbank=pbank: e.tensor_add(
                                                    od3[tcc][:], xr3[tcc][:], pbank[:, :]))
                        kb.wait("scalar", "VE", t_ep3[(oci, tcc)])
                        t_ost3[(oci, tcc)] = dma(kb, "scalar", f"SD{tcc}",
                                                 x_dram[128 * oci:128 * oci + 128,
                                                        TC * tcc:TC * tcc + TC],
                                                 od3[tcc][:])

        kb.barrier()
        if nphase >= 4:
            emit_ffn(x_dram, out_ext, "f4")
        else:
            with ExitStack() as pf:
                cp = kb.sb(pf, "cp", [128, NIT * TC], F32)
                t_s = 0
                for tcc in range(2):
                    kb.wait("sync", "SD0", t_s)
                    t_l = dma(kb, "sync", "LD", cp[:].rearrange("p (i t) -> p i t", t=TC),
                              x_dram[:, tcc * TC:(tcc + 1) * TC].rearrange("(i p) t -> p i t", p=128))
                    kb.wait("scalar", "LD", t_l)
                    t_s = dma(kb, "scalar", "SD0",
                              out_ext[:, tcc * TC:(tcc + 1) * TC].rearrange("(i p) t -> p i t", p=128),
                              cp[:].rearrange("p (i t) -> p i t", t=TC))
                    kb.wait("scalar", "SD0", t_s)

        kb.wait("sync", "SD0", kb.cnt["SD0"])
        kb.wait("sync", "SD1", kb.cnt["SD1"])
        kb.wait("sync", "GD", kb.cnt["GD"])

        assert not kb.bg_vector, f"{len(kb.bg_vector)} background ops never pumped"

        with nc.Block() as block:
            @block.sync
            def _(e):
                for f in kb.streams["sync"]:
                    f(e)

            @block.tensor
            def _(e):
                for f in kb.streams["tensor"]:
                    f(e)

            @block.vector
            def _(e):
                for f in kb.streams["vector"]:
                    f(e)

            @block.scalar
            def _(e):
                for f in kb.streams["scalar"]:
                    f(e)

            @block.gpsimd
            def _(e):
                for f in kb.streams["gpsimd"]:
                    f(e)

    return nc


def kernel(**inputs):
    in_maps = shard_inputs(inputs)
    if "nc" not in _NC_CACHE:
        _NC_CACHE["nc"] = build_program(4)
    nc = _NC_CACHE["nc"]
    res = run_bass_kernel_spmd(nc, in_maps, core_ids=list(range(NCORE)))
    xt = np.concatenate([np.asarray(res.results[c]["out"]) for c in range(NCORE)], axis=1)
    out = np.ascontiguousarray(xt.T).reshape(2, S, D).astype(np.float32)
    return out


# version 10 (final)
# speedup vs baseline: 1.0011x; 1.0011x over previous
import os
import numpy as np
import ml_dtypes
from contextlib import ExitStack

import concourse.bass as bass
import concourse.mybir as mybir
from concourse.bass_utils import run_bass_kernel_spmd

F32 = mybir.dt.float32
BF16 = mybir.dt.bfloat16
I32 = mybir.dt.int32
AF = mybir.ActivationFunctionType
ALU = mybir.AluOpType
AX = mybir.AxisListType
ET = mybir.EngineType

NCORE = 8
D = 4096
NIT = 32
TOK = 1024
TC = 512
INTER = 11008
NI = 86
HD = 128
S = 4096
RG = [[0, 1, 2, 3, 4, 5, 6, 7]]
BF = ml_dtypes.bfloat16

_NC_CACHE = {}


def _sb(x):
    return np.ascontiguousarray(x)


def make_host_constants():
    c = {}
    c["c_ident"] = np.eye(128, dtype=BF)
    R = np.zeros((128, 128), np.float32)
    for d in range(64):
        R[d, d + 64] = -1.0
        R[d + 64, d] = 1.0
    c["c_rott"] = _sb(R.T.astype(BF))
    m = np.where(np.tril(np.ones((128, 128), bool)), 0.0, -30000.0)
    c["c_mask"] = m.astype(np.float32)
    inv_freq = 1.0 / (10000.0 ** (np.arange(0, HD, 2, dtype=np.float64) / HD))
    freqs = np.arange(S, dtype=np.float64)[:, None] * inv_freq[None, :]
    emb = np.concatenate([freqs, freqs], axis=1)
    cos = np.cos(emb).T.astype(np.float32)
    sin = np.sin(emb).T.astype(np.float32)
    sc = 1.0 / np.sqrt(HD)
    c["c_cosq"] = _sb(cos * sc)
    c["c_sinq"] = _sb(sin * sc)
    c["c_cosk"] = _sb(cos)
    c["c_sink"] = _sb(sin)
    sh8 = np.zeros((128, 8), np.int32)
    for j in range(8):
        sh8[:, j] = 4 * j
    c["c_sh8"] = sh8
    shp = np.zeros((128, 4), np.int32)
    for mm_ in range(4):
        shp[0:64, mm_] = 8 * mm_
        shp[64:128, mm_] = 8 * mm_ + 4
    c["c_shp"] = shp
    c["c_m15"] = np.full((128, 1), 15, np.int32)
    return c


def shard_inputs(inputs):
    cst = make_host_constants()
    x = inputs["L0_attn_a_x"]
    W = _sb(np.asarray(inputs["L0_attn_a_weight"], np.float32))
    bias_t = _sb(np.asarray(inputs["L0_attn_a_bias"], np.float32).reshape(32, 128).T)
    sq = np.asarray(inputs["L0_attn_b_scales_q"], np.float32)
    sk = np.asarray(inputs["L0_attn_b_scales_k"], np.float32)
    sv = np.asarray(inputs["L0_attn_b_scales_v"], np.float32)
    so = np.asarray(inputs["L0_attn_b_scales_o"], np.float32)
    qq = np.asarray(inputs["L0_attn_b_qw_q"], np.int32)
    qk = np.asarray(inputs["L0_attn_b_qw_k"], np.int32)
    qv = np.asarray(inputs["L0_attn_b_qw_v"], np.int32)
    qo = np.asarray(inputs["L0_attn_b_qw_o"], np.int32)
    gq = np.asarray(inputs["L0_ffn_gate_qweight"], np.int32)
    gs = np.asarray(inputs["L0_ffn_gate_scales"], np.float32)
    gz = np.asarray(inputs["L0_ffn_gate_qzeros"], np.int32)
    uq = np.asarray(inputs["L0_ffn_up_qweight"], np.int32)
    us = np.asarray(inputs["L0_ffn_up_scales"], np.float32)
    uz = np.asarray(inputs["L0_ffn_up_qzeros"], np.int32)
    dq = np.asarray(inputs["L0_ffn_down_qweight"], np.int32)
    ds = np.asarray(inputs["L0_ffn_down_scales"], np.float32)
    dz = np.asarray(inputs["L0_ffn_down_qzeros"], np.int32)

    in_maps = []
    for c in range(NCORE):
        b, q4 = c // 4, c % 4
        im = dict(cst)
        im["xin"] = _sb(np.asarray(x[b, q4 * TOK:(q4 + 1) * TOK, :], np.float32).T)
        im["aw"] = W
        im["abias"] = bias_t
        im["qwq"] = _sb(qq[:, 512 * c: 512 * (c + 1)])
        im["qwk"] = _sb(qk[:, 128 * c: 128 * (c + 1)])
        im["qwv"] = _sb(qv[:, 128 * c: 128 * (c + 1)])
        im["sclq"] = _sb(sq[:, 512 * c: 512 * (c + 1)])
        im["sclk"] = _sb(sk[:, 128 * c: 128 * (c + 1)])
        im["sclv"] = _sb(sv[:, 128 * c: 128 * (c + 1)])
        im["qwo"] = _sb(qo[512 * c: 512 * (c + 1), :])
        im["sclo"] = _sb(so[4 * c: 4 * (c + 1), :])
        im["gqw"] = _sb(gq[64 * c: 64 * (c + 1), :])
        im["gsc"] = _sb(gs[4 * c: 4 * (c + 1), :])
        im["gzr"] = _sb(gz[4 * c: 4 * (c + 1), :])
        im["uqw"] = _sb(uq[64 * c: 64 * (c + 1), :])
        im["usc"] = _sb(us[4 * c: 4 * (c + 1), :])
        im["uzr"] = _sb(uz[4 * c: 4 * (c + 1), :])
        im["dqw"] = _sb(dq[:, 512 * c: 512 * (c + 1)])
        im["dsc"] = _sb(ds[:, 512 * c: 512 * (c + 1)])
        im["dzr"] = _sb(dz[:, 64 * c: 64 * (c + 1)])
        im["ggi"] = _sb(np.asarray(inputs["L0_ffn_gate_g_idx"], np.int32).reshape(1, -1))
        im["ugi"] = _sb(np.asarray(inputs["L0_ffn_up_g_idx"], np.int32).reshape(1, -1))
        im["dgi"] = _sb(np.asarray(inputs["L0_ffn_down_g_idx"], np.int32).reshape(1, -1))
        im["bidx"] = np.array([[b]], np.int32)
        im["qidx"] = np.array([[q4]], np.int32)
        in_maps.append(im)
    return in_maps


class KB:
    def __init__(self, nc, ctx):
        self.nc = nc
        self.ctx = ctx
        self.streams = {k: [] for k in ("sync", "tensor", "vector", "scalar", "gpsimd")}
        self.sems = {}
        self.cnt = {}
        self.bg_vector = []

    def sem(self, name):
        self.sems[name] = self.ctx.enter_context(self.nc.semaphore(name))
        self.cnt[name] = 0

    def tok(self, name, n):
        self.cnt[name] += n
        return self.cnt[name]

    def emit(self, stream, fn):
        self.streams[stream].append(fn)

    def wait(self, stream, sem, val):
        if val is None or val <= 0:
            return
        h = self.sems[sem]
        self.emit(stream, lambda e, h=h, v=val: e.wait_ge(h, v))

    def pump_bg(self, n):
        for _ in range(min(n, len(self.bg_vector))):
            self.emit("vector", self.bg_vector.pop(0))

    _sbuf_n = 0

    def sb(self, scope, name, shape, dtype):
        KB._sbuf_n += 1
        return scope.enter_context(self.nc.sbuf_tensor(f"{name}_{KB._sbuf_n}", shape, dtype))

    def barrier(self, exclude=(), flush_bg=True):
        if flush_bg:
            self.pump_bg(10 ** 9)
        snap = dict(self.cnt)
        for st in self.streams:
            for sname, v in snap.items():
                if sname in exclude or (not flush_bg and sname == "BG"):
                    continue
                self.wait(st, sname, v)


def dma(kb, stream, sem, out, in_, **kw):
    t = kb.tok(sem, 16)
    h = kb.sems[sem]
    kb.emit(stream, lambda e, o=out, i=in_, h=h, kw=kw: e.dma_start(out=o, in_=i, **kw).then_inc(h, 16))
    return t


def inc(kb, stream, sem, fn):
    t = kb.tok(sem, 1)
    h = kb.sems[sem]
    kb.emit(stream, lambda e, f=fn, h=h: f(e).then_inc(h, 1))
    return t


def op(kb, stream, fn):
    kb.emit(stream, fn)


def build_program(nphase=4, detect_races=False):
    nc = bass.Bass("TRN2", target_bir_lowering=False, debug=False,
                   detect_race_conditions=detect_races)

    def din(name, shape, dtype):
        return nc.dram_tensor(name, shape, dtype, kind="ExternalInput")

    xin = din("xin", [D, TOK], F32)
    aw = din("aw", [D, D], F32)
    abias = din("abias", [128, 32], F32)
    qwq = din("qwq", [D, 512], I32)
    qwk = din("qwk", [D, 128], I32)
    qwv = din("qwv", [D, 128], I32)
    sclq = din("sclq", [32, 512], F32)
    sclk = din("sclk", [32, 128], F32)
    sclv = din("sclv", [32, 128], F32)
    qwo = din("qwo", [512, D], I32)
    sclo = din("sclo", [4, D], F32)
    gqw = din("gqw", [64, INTER], I32)
    gsc = din("gsc", [4, INTER], F32)
    gzr = din("gzr", [4, 1376], I32)
    uqw = din("uqw", [64, INTER], I32)
    usc = din("usc", [4, INTER], F32)
    uzr = din("uzr", [4, 1376], I32)
    dqw = din("dqw", [1376, 512], I32)
    dsc = din("dsc", [86, 512], F32)
    dzr = din("dzr", [86, 64], I32)
    ggi = din("ggi", [1, D], I32)
    ugi = din("ugi", [1, D], I32)
    dgi = din("dgi", [1, INTER], I32)
    bidx = din("bidx", [1, 1], I32)
    qidx = din("qidx", [1, 1], I32)
    c_ident = din("c_ident", [128, 128], BF16)
    c_rott = din("c_rott", [128, 128], BF16)
    c_mask = din("c_mask", [128, 128], F32)
    c_cosq = din("c_cosq", [128, S], F32)
    c_sinq = din("c_sinq", [128, S], F32)
    c_cosk = din("c_cosk", [128, S], F32)
    c_sink = din("c_sink", [128, S], F32)
    c_sh8 = din("c_sh8", [128, 8], I32)
    c_shp = din("c_shp", [128, 4], I32)
    c_m15 = din("c_m15", [128, 1], I32)

    out_ext = nc.dram_tensor("out", [D, TOK], F32, kind="ExternalOutput")

    x_dram = nc.dram_tensor("x_dram", [D, TOK], F32)
    g_contrib = nc.dram_tensor("g_contrib", [512, INTER], BF16)
    u_contrib = nc.dram_tensor("u_contrib", [512, INTER], BF16)
    d_contrib = nc.dram_tensor("d_contrib", [INTER, 512], BF16)
    o_contrib = nc.dram_tensor("o_contrib", [512, D], BF16)
    gate_bf = nc.dram_tensor("gate_bf", [D, INTER], BF16)
    up_bf = nc.dram_tensor("up_bf", [D, INTER], BF16)
    down_bf = nc.dram_tensor("down_bf", [8 * INTER, 512], BF16)
    wo_bf = nc.dram_tensor("wo_bf", [D, D], BF16)
    wq_bf = nc.dram_tensor("wq_bf", [D, 512], BF16)
    wk_bf = nc.dram_tensor("wk_bf", [D, 128], BF16)
    wv_bf = nc.dram_tensor("wv_bf", [D, 128], BF16)
    zg_scr = nc.dram_tensor("zg_scr", [4, INTER], F32)
    zu_scr = nc.dram_tensor("zu_scr", [4, INTER], F32)
    zd_scr = nc.dram_tensor("zd_scr", [86, 512], F32)
    xn_contrib = [nc.dram_tensor(f"xn_contrib{i}", [D, TC], BF16) for i in range(2)]
    xn_ag = [nc.dram_tensor(f"xn_ag{i}", [8 * D, TC], BF16) for i in range(2)]
    q_scr = nc.dram_tensor("q_scr", [1024, S], BF16)
    k_scr = nc.dram_tensor("k_scr", [256, S], BF16)
    v_scr = nc.dram_tensor("v_scr", [256, S], BF16)
    ctx_contrib = nc.dram_tensor("ctx_contrib", [1024, S], BF16)
    ctx_ag = nc.dram_tensor("ctx_ag", [8 * 1024, S], BF16)
    kdbg = os.environ.get("KDBG")
    if kdbg:
        dbg_xn = nc.dram_tensor("dbg_xn", [128, NIT * TOK], BF16, kind="ExternalOutput")
        dbg_wt = nc.dram_tensor("dbg_wt", [128, NIT * 128], BF16, kind="ExternalOutput")
        dbg_ct = nc.dram_tensor("dbg_ct", [128, NIT * 1024], BF16, kind="ExternalOutput")
        dbg_g = nc.dram_tensor("dbg_g", [8 * INTER, 512], BF16, kind="ExternalOutput")
        dbg_z = nc.dram_tensor("dbg_z", [86, 512], F32, kind="ExternalOutput")
        dbg_zp = nc.dram_tensor("dbg_zp", [86, 64], I32, kind="ExternalOutput")
        dbg_zg = nc.dram_tensor("dbg_zg", [4, INTER], F32, kind="ExternalOutput")

    with ExitStack() as CTX:
        kb = KB(nc, CTX)
        for s_ in ("LD", "LW0", "LW1", "LU0", "LU1", "LDN0", "LDN1", "LX0", "LX1", "LA0", "LA1", "GD", "SD0", "SD1", "CC", "TE", "VE", "SE", "BG"):
            kb.sem(s_)

        pb = [CTX.enter_context(nc.psum_tensor(f"pb{i}", [128, 512], F32))
              for i in range(8)]

        ident = kb.sb(CTX, "ident", [128, 128], BF16)
        rott = kb.sb(CTX, "rott", [128, 128], BF16)
        maskt = kb.sb(CTX, "maskt", [128, 128], F32)
        biast = kb.sb(CTX, "biast", [128, 32], F32)
        onesc = kb.sb(CTX, "onesc", [128, 1], BF16)
        onesr = kb.sb(CTX, "onesr", [1, 128], F32)
        stats = kb.sb(CTX, "stats", [1, 1024], F32)
        sh8 = kb.sb(CTX, "sh8", [128, 8], I32)
        shp = kb.sb(CTX, "shp", [128, 4], I32)
        m15 = kb.sb(CTX, "m15", [128, 1], I32)
        mrep = kb.sb(CTX, "mrep", [128, 512], F32)
        rrep = kb.sb(CTX, "rrep", [128, 512], F32)
        nmax = kb.sb(CTX, "nmax", [128, 1], F32)
        rsum = kb.sb(CTX, "rsum", [128, 1], F32)
        rinv = kb.sb(CTX, "rinv", [128, 1], F32)
        dmt = kb.sb(CTX, "dmt", [128, 128], BF16)
        scal1 = kb.sb(CTX, "scal1", [1, 512], F32)
        ridb = kb.sb(CTX, "ridb", [1, 1], I32)
        ridq = kb.sb(CTX, "ridq", [1, 1], I32)
        gidx_t = kb.sb(CTX, "gidx_t", [1, 128], I32)

        t_const = 0
        for dst, src in ((ident, c_ident), (rott, c_rott), (maskt, c_mask),
                         (biast, abias), (sh8, c_sh8), (shp, c_shp), (m15, c_m15),
                         (ridb, bidx), (ridq, qidx)):
            t_const = dma(kb, "sync", "LD", dst[:], src[:])
        for gsrc in (ggi, ugi, dgi):
            t_const = dma(kb, "sync", "LD", gidx_t[0:1, 0:128], gsrc[0:1, 0:128])
        op(kb, "vector", lambda e: e.memset(onesc[:], 1.0))
        t_ones = inc(kb, "vector", "VE", lambda e: e.memset(onesr[:], 1.0))

        ln_prev = {"xn": 0, "ssq": 0, "mr": 0, "sq": 0}

        def emit_ln(scope, src_dram, tcc, xn, off, vec_war=0):
            xc = kb.sb(scope, "xc", [128, NIT * TC], F32)
            ba = kb.sb(scope, "ba", [128, NIT * TC], BF16)
            kb.wait("sync", "VE", ln_prev["xn"])
            kb.wait("sync", "SE", ln_prev["sq"])
            t_x = dma(kb, "sync", "LD", xc[:].rearrange("p (i t) -> p i t", t=TC),
                      src_dram[:, tcc * TC:(tcc + 1) * TC].rearrange("(i p) t -> p i t", p=128))
            kb.wait("scalar", "LD", t_x)
            kb.wait("scalar", "TE", ln_prev["ssq"])
            t_xb = inc(kb, "scalar", "SE", lambda e: e.activation(ba[:], xc[:], AF.Copy))
            kb.wait("tensor", "SE", t_xb)
            kb.wait("tensor", "VE", max(t_ones, ln_prev["mr"]))
            t_sum = None
            for it in range(NIT):
                t_sum = inc(kb, "tensor", "TE",
                            lambda e, it=it: e.matmul(
                                pb[6][0:1, :], onesc[:], ba[:, it * TC:(it + 1) * TC],
                                start=(it == 0), stop=(it == NIT - 1)))
            kb.wait("scalar", "TE", t_sum)
            t_sq = inc(kb, "scalar", "SE", lambda e: e.activation(ba[:], xc[:], AF.Square))
            kb.wait("tensor", "SE", t_sq)
            t_ssq = None
            for it in range(NIT):
                t_ssq = inc(kb, "tensor", "TE",
                            lambda e, it=it: e.matmul(
                                pb[7][0:1, :], onesc[:], ba[:, it * TC:(it + 1) * TC],
                                start=(it == 0), stop=(it == NIT - 1)))
            kb.wait("vector", "TE", t_ssq)
            op(kb, "vector", lambda e: e.tensor_scalar_mul(stats[0:1, 0:512], pb[6][0:1, :], 1.0 / D))
            op(kb, "vector", lambda e: e.tensor_scalar_mul(scal1[0:1, :], pb[7][0:1, :], 1.0 / D))
            op(kb, "vector", lambda e: e.tensor_tensor(stats[0:1, 512:1024],
                                                       stats[0:1, 0:512], stats[0:1, 0:512], ALU.mult))
            op(kb, "vector", lambda e: e.tensor_sub(scal1[0:1, :], scal1[0:1, :], stats[0:1, 512:1024]))
            t_var = inc(kb, "vector", "VE",
                        lambda e: e.tensor_scalar_add(scal1[0:1, :], scal1[0:1, :], 1e-5))
            kb.wait("scalar", "VE", t_var)
            op(kb, "scalar", lambda e: e.activation(scal1[0:1, :], scal1[0:1, :], AF.Ln))
            t_rstd = inc(kb, "scalar", "SE",
                         lambda e: e.activation(stats[0:1, 512:1024], scal1[0:1, :], AF.Exp, scale=-0.5))
            kb.wait("tensor", "SE", t_rstd)
            kb.wait("tensor", "VE", t_var)
            op(kb, "tensor", lambda e: e.matmul(pb[6][:, :], onesr[:], stats[0:1, 0:512],
                                                start=True, stop=True))
            t_rep = inc(kb, "tensor", "TE",
                        lambda e: e.matmul(pb[7][:, :], onesr[:], stats[0:1, 512:1024],
                                           start=True, stop=True))
            kb.wait("vector", "TE", t_rep)
            op(kb, "vector", lambda e: e.tensor_copy(mrep[:], pb[6][:, :]))
            t_mr = inc(kb, "vector", "VE", lambda e: e.tensor_copy(rrep[:], pb[7][:, :]))
            op(kb, "vector", lambda e: e.tensor_tensor(
                xc[:].rearrange("p (i t) -> p i t", t=TC),
                xc[:].rearrange("p (i t) -> p i t", t=TC),
                mrep[:].rearrange("p (a t) -> p a t", a=1).broadcast_to([128, NIT, TC]),
                ALU.subtract))
            kb.wait("vector", "TE", vec_war)
            t_xn = inc(kb, "vector", "VE", lambda e, off=off: e.tensor_tensor(
                xn[:, off:off + NIT * TC].rearrange("p (i t) -> p i t", t=TC),
                xc[:].rearrange("p (i t) -> p i t", t=TC),
                rrep[:].rearrange("p (a t) -> p a t", a=1).broadcast_to([128, NIT, TC]),
                ALU.mult))
            ln_prev.update(xn=t_xn, ssq=t_ssq, mr=t_mr, sq=t_sq)
            return t_xn

        W0 = 688
        p0 = ExitStack()
        dq_qw = kb.sb(p0, "dq_qw", [128, W0], I32)
        dq_w4 = kb.sb(p0, "dq_w4", [128, W0], I32)
        dq_f = kb.sb(p0, "dq_f", [128, W0], F32)
        dq_s = kb.sb(p0, "dq_s", [128, W0], F32)
        dq_z = kb.sb(p0, "dq_z", [128, W0], F32)
        dq_o = kb.sb(p0, "dq_o", [128, W0], BF16)
        dq_zp = kb.sb(p0, "dq_zp", [128, 86], I32)
        dq_zj = kb.sb(p0, "dq_zj", [128, 86], I32)
        dq_zi = kb.sb(p0, "dq_zi", [128, 688], I32)
        dq_zf = kb.sb(p0, "dq_zf", [128, 688], F32)

        def bg(fn):
            kb.bg_vector.append(fn)

        def bg_tok(fn):
            t = kb.tok("BG", 1)
            h = kb.sems["BG"]
            kb.bg_vector.append(lambda e, f=fn, h=h: f(e).then_inc(h, 1))
            return t

        def bg_wait(sem, t):
            if t:
                h = kb.sems[sem]
                kb.bg_vector.append(lambda e, h=h, v=t: e.wait_ge(h, v))

        ag_toks = {}
        t_slice_done = 0
        t_store_prev = 0
        t_zchunk_prev = 0

        def emit_zeros(zr_in, grp, npk_total, z_scr):
            nonlocal t_zchunk_prev
            nchunk = (npk_total + 85) // 86
            for ch in range(nchunk):
                c0 = 86 * ch
                cw = min(86, npk_total - c0)
                kb.wait("gpsimd", "BG", t_zchunk_prev)
                t_l = dma(kb, "gpsimd", "GD", dq_zp[0:grp, 0:cw], zr_in[:, c0:c0 + cw])
                if kdbg and grp == 86:
                    kb.wait("gpsimd", "GD", t_l)
                    dma(kb, "gpsimd", "GD", dbg_zp[:], dq_zp[0:86, 0:64])
                bg_wait("GD", t_l)
                for jn in range(8):
                    bg(lambda e, jn=jn, g=grp, n=cw: e.tensor_scalar(
                        dq_zi[0:g, 0:8 * n].rearrange("p (c e) -> p c e", e=8)[:, :, jn:jn + 1]
                        .rearrange("p c e -> p (c e)"),
                        dq_zp[0:g, 0:n], 4 * jn, 15,
                        ALU.logical_shift_right, ALU.bitwise_and))
                bg(lambda e: e.tensor_copy(dq_zj[:, 0:86], dq_zp[:, 0:86]))
                bg(lambda e: e.tensor_copy(dq_zj[:, 0:86], dq_zp[:, 0:86]))
                bg(lambda e, g=grp, n=cw: e.tensor_single_scalar(
                    dq_zf[0:g, 0:8 * n], dq_zi[0:g, 0:8 * n], 1.0, ALU.add))
                t_z = bg_tok(lambda e: e.tensor_copy(dq_zj[:, 0:86], dq_zp[:, 0:86]))
                kb.wait("gpsimd", "BG", t_z)
                t_zst = dma(kb, "gpsimd", "GD", z_scr[:, 8 * c0:8 * c0 + 8 * cw],
                            dq_zf[0:grp, 0:8 * cw])
                t_zchunk_prev = t_z
            return t_zst

        def gate_like(qw_in, sc_in, zr_in, z_scr, contrib, agname, out_ag):
            nonlocal t_slice_done, t_store_prev
            t_zst = emit_zeros(zr_in, 4, 1376, z_scr)
            NSL, SLW = 16, 688
            t_st = 0
            for sl in range(NSL):
                c0 = sl * SLW
                kb.wait("gpsimd", "BG", t_slice_done)
                dma(kb, "gpsimd", "GD", dq_qw[0:64, 0:SLW], qw_in[:, c0:c0 + SLW])
                dma(kb, "gpsimd", "GD", dq_qw[64:128, 0:SLW], qw_in[:, c0:c0 + SLW])
                if sl == 0:
                    kb.wait("gpsimd", "GD", t_zst)
                t_sz = None
                for g in range(4):
                    for half in range(2):
                        base = half * 64 + 16 * g
                        dma(kb, "gpsimd", "GD", dq_s[base:base + 16, 0:SLW],
                            sc_in[g:g + 1, c0:c0 + SLW].broadcast_to([16, SLW]))
                        t_sz = dma(kb, "gpsimd", "GD", dq_z[base:base + 16, 0:SLW],
                                   z_scr[g:g + 1, c0:c0 + SLW].broadcast_to([16, SLW]))
                bg_wait("GD", t_sz)
                for mpair in range(4):
                    j = 2 * mpair
                    bg(lambda e, m=mpair: e.tensor_scalar(
                        dq_w4[0:64, 0:688], dq_qw[0:64, 0:688], 8 * m, 15,
                        ALU.logical_shift_right, ALU.bitwise_and))
                    bg(lambda e, m=mpair: e.tensor_scalar(
                        dq_w4[64:128, 0:688], dq_qw[64:128, 0:688], 8 * m + 4, 15,
                        ALU.logical_shift_right, ALU.bitwise_and))
                    bg(lambda e: e.tensor_tensor(dq_f[:, 0:688], dq_w4[:, 0:688],
                                                 dq_z[:, 0:688], ALU.subtract))
                    bg_wait("GD", t_store_prev)
                    t_d = bg_tok(lambda e: e.tensor_tensor(dq_o[:, 0:688], dq_f[:, 0:688],
                                                           dq_s[:, 0:688], ALU.mult))
                    kb.wait("gpsimd", "BG", t_d)
                    dma(kb, "gpsimd", "GD",
                        contrib[:, c0:c0 + SLW].rearrange("(r e) n -> e r n", e=8)[j:j + 1, :, :],
                        dq_o[0:64, 0:688])
                    t_st = dma(kb, "gpsimd", "GD",
                               contrib[:, c0:c0 + SLW].rearrange("(r e) n -> e r n", e=8)[j + 1:j + 2, :, :],
                               dq_o[64:128, 0:688])
                    t_store_prev = t_st
                    t_slice_done = t_d
            kb.wait("gpsimd", "GD", t_st)
            ag_toks[agname] = inc(kb, "gpsimd", "CC",
                                  lambda e, i=contrib, o=out_ag: e.collective_compute(
                                      "AllGather", ALU.bypass, replica_groups=RG,
                                      ins=[i.ap().opt()], outs=[o.ap().opt()]))

        gate_like(gqw, gsc, gzr, zg_scr, g_contrib, "gate", gate_bf)
        gate_like(uqw, usc, uzr, zu_scr, u_contrib, "up", up_bf)

        t_zst = emit_zeros(dzr, 86, 64, zd_scr)
        t_st = 0
        for blk in range(11):
            r0 = 128 * blk
            np_ = min(128, 1376 - r0)
            kb.wait("gpsimd", "BG", t_slice_done)
            dma(kb, "gpsimd", "GD", dq_qw[0:np_, 0:512], dqw[r0:r0 + np_, :])
            if blk == 0:
                kb.wait("gpsimd", "GD", t_zst)
            t_sz = None
            ng = (np_ + 15) // 16
            for g in range(ng):
                base = 16 * g
                cnt = min(16, np_ - base)
                grow = 8 * blk + g
                dma(kb, "gpsimd", "GD", dq_s[base:base + cnt, 0:512],
                    dsc[grow:grow + 1, :].broadcast_to([cnt, 512]))
                t_sz = dma(kb, "gpsimd", "GD", dq_z[base:base + cnt, 0:512],
                           zd_scr[grow:grow + 1, :].broadcast_to([cnt, 512]))
            bg_wait("GD", t_sz)
            for j in range(8):
                bg(lambda e, j=j, n=np_: e.tensor_scalar(
                    dq_w4[0:n, 0:512], dq_qw[0:n, 0:512], 4 * j, 15,
                    ALU.logical_shift_right, ALU.bitwise_and))
                bg(lambda e, n=np_: e.tensor_tensor(dq_f[0:n, 0:512], dq_w4[0:n, 0:512],
                                                    dq_z[0:n, 0:512], ALU.subtract))
                bg_wait("GD", t_store_prev)
                t_d = bg_tok(lambda e, n=np_: e.tensor_tensor(dq_o[0:n, 0:512], dq_f[0:n, 0:512],
                                                              dq_s[0:n, 0:512], ALU.mult))
                kb.wait("gpsimd", "BG", t_d)
                t_st = dma(kb, "gpsimd", "GD",
                           d_contrib.rearrange("(r e) n -> e r n", e=8)[j:j + 1, r0:r0 + np_, :],
                           dq_o[0:np_, 0:512])
                t_store_prev = t_st
                t_slice_done = t_d
                kb.wait("gpsimd", "GD", t_st)
        ag_toks["down"] = inc(kb, "gpsimd", "CC",
                              lambda e: e.collective_compute(
                                  "AllGather", ALU.bypass, replica_groups=RG,
                                  ins=[d_contrib.ap().opt()], outs=[down_bf.ap().opt()]))

        t_st = 0
        for blk in range(4):
            for cs in range(8):
                kb.wait("gpsimd", "BG", t_slice_done)
                t_q = dma(kb, "gpsimd", "GD", dq_qw[:, 0:512],
                          qwo[128 * blk:128 * blk + 128, 512 * cs:512 * cs + 512])
                t_s = dma(kb, "gpsimd", "GD", dq_s[:, 0:512],
                          sclo[blk:blk + 1, 512 * cs:512 * cs + 512].broadcast_to([128, 512]))
                bg_wait("GD", t_s)
                bg(lambda e: e.tensor_single_scalar(dq_f[:, 0:512], dq_qw[:, 0:512],
                                                    8.0, ALU.subtract))
                bg_wait("GD", t_store_prev)
                t_d = bg_tok(lambda e: e.tensor_tensor(dq_o[:, 0:512], dq_f[:, 0:512],
                                                       dq_s[:, 0:512], ALU.mult))
                kb.wait("gpsimd", "BG", t_d)
                t_st = dma(kb, "gpsimd", "GD",
                           o_contrib[128 * blk:128 * blk + 128, 512 * cs:512 * cs + 512],
                           dq_o[:, 0:512])
                t_store_prev = t_st
                t_slice_done = t_d
        kb.wait("gpsimd", "GD", t_st)
        ag_toks["wo"] = inc(kb, "gpsimd", "CC",
                            lambda e: e.collective_compute(
                                "AllGather", ALU.bypass, replica_groups=RG,
                                ins=[o_contrib.ap().opt()], outs=[wo_bf.ap().opt()]))

        for (qw_in, sc_in, w_out, wd_) in ((qwq, sclq, wq_bf, 512),
                                           (qwk, sclk, wk_bf, 128),
                                           (qwv, sclv, wv_bf, 128)):
            for it in range(NIT):
                kb.wait("gpsimd", "BG", t_slice_done)
                t_l = dma(kb, "gpsimd", "GD", dq_qw[:, 0:wd_], qw_in[128 * it:128 * it + 128, :])
                t_s = dma(kb, "gpsimd", "GD", dq_s[:, 0:wd_],
                          sc_in[it:it + 1, :].broadcast_to([128, wd_]))
                bg_wait("GD", t_s)
                bg(lambda e, w=wd_: e.tensor_single_scalar(dq_f[:, 0:w], dq_qw[:, 0:w],
                                                           8.0, ALU.subtract))
                bg_wait("GD", t_store_prev)
                t_d = bg_tok(lambda e, w=wd_: e.tensor_tensor(dq_o[:, 0:w], dq_f[:, 0:w],
                                                              dq_s[:, 0:w], ALU.mult))
                kb.wait("gpsimd", "BG", t_d)
                t_store_prev = dma(kb, "gpsimd", "GD", w_out[128 * it:128 * it + 128, :],
                                   dq_o[:, 0:wd_])
                t_slice_done = t_d
        t_qkvw_done = kb.cnt["GD"]

        if nphase >= 1:
            with ExitStack() as p1:
                xn1 = kb.sb(p1, "xn1", [128, NIT * TOK], BF16)
                with ExitStack() as lns:
                    emit_ln(lns, xin, 0, xn1, 0)
                kb.pump_bg(120)
                with ExitStack() as lns:
                    t_xn1 = emit_ln(lns, xin, 1, xn1, NIT * TC)
                kb.pump_bg(120)
                kb.wait("sync", "VE", t_xn1)
                kb.wait("sync", "SE", ln_prev["sq"])
                kb.wait("sync", "TE", ln_prev["ssq"])
                wrow_f = [kb.sb(p1, f"wrow_f{i}", [128, D], F32) for i in range(2)]
                wrow_b = [kb.sb(p1, f"wrow_b{i}", [128, D], BF16) for i in range(2)]
                wT = kb.sb(p1, "wT", [128, NIT * 128], BF16)
                xrl = [kb.sb(p1, f"xrl{i}", [128, TC], F32) for i in range(2)]
                ost = [kb.sb(p1, f"ost{i}", [128, TC], F32) for i in range(2)]
                kb.wait("tensor", "VE", t_xn1)
                t_cast = {}
                t_tr_done = {}
                t_ep = {}
                t_ost_st = {}
                for oc in range(NIT):
                    wp = oc % 2
                    kb.wait("sync", "SE", t_cast.get(oc - 2, 0))
                    t_wl = None
                    for half in range(2):
                        t_wl = dma(kb, "sync", f"LW{wp}", wrow_f[wp][:, 2048 * half:2048 * half + 2048],
                                   aw[128 * oc:128 * oc + 128, 2048 * half:2048 * half + 2048])
                    kb.wait("scalar", f"LW{wp}", t_wl)
                    kb.wait("scalar", "TE", t_tr_done.get(oc - 2, 0))
                    t_cast[oc] = inc(kb, "scalar", "SE",
                                     lambda e, wp=wp: e.activation(
                                         wrow_b[wp][:], wrow_f[wp][:], AF.Copy))
                    kb.wait("tensor", "SE", t_cast[oc])
                    t_ev = {}
                    for g4 in range(8):
                        tb = pb[2] if g4 % 2 == 0 else pb[3]
                        kb.wait("tensor", "VE", t_ev.get(g4 - 2, 0))
                        t_tp = None
                        for k4 in range(4):
                            it = 4 * g4 + k4
                            t_tp = inc(kb, "tensor", "TE",
                                       lambda e, it=it, k4=k4, tb=tb, wp=wp: e.matmul(
                                           tb[:, 128 * k4:128 * k4 + 128],
                                           wrow_b[wp][:, 128 * it:128 * it + 128], ident[:],
                                           start=True, stop=True))
                        kb.wait("vector", "TE", t_tp)
                        t_ev[g4] = inc(kb, "vector", "VE",
                                       lambda e, g4=g4, tb=tb: e.tensor_copy(
                                           wT[:, 512 * g4:512 * g4 + 512], tb[:, :]))
                    t_tr_done[oc] = kb.cnt["TE"]
                    kb.wait("tensor", "VE", t_ev[7])
                    for tcc in range(2):
                        kb.wait("sync", "VE", t_ep.get((oc - 1, tcc), 0))
                        t_xr = dma(kb, "sync", f"LX{tcc}", xrl[tcc][:],
                                   xin[128 * oc:128 * oc + 128, TC * tcc:TC * tcc + TC])
                        pbank = pb[tcc]
                        kb.wait("tensor", "VE", t_ep.get((oc - 1, tcc), 0))
                        t_mm = None
                        for it in range(NIT):
                            t_mm = inc(kb, "tensor", "TE",
                                       lambda e, it=it, tcc=tcc, pbank=pbank: e.matmul(
                                           pbank[:, :], wT[:, 128 * it:128 * it + 128],
                                           xn1[:, NIT * TC * tcc + TC * it:NIT * TC * tcc + TC * it + TC],
                                           start=(it == 0), stop=(it == NIT - 1)))
                        kb.wait("vector", "TE", t_mm)
                        kb.wait("vector", f"LX{tcc}", t_xr)
                        kb.wait("vector", f"SD{tcc}", t_ost_st.get((oc - 1, tcc), 0))
                        t_ep[(oc, tcc)] = inc(kb, "vector", "VE",
                                              lambda e, oc=oc, tcc=tcc, pbank=pbank: e.scalar_tensor_tensor(
                                                  ost[tcc][:], pbank[:, :], biast[:, oc:oc + 1],
                                                  xrl[tcc][:], ALU.add, ALU.add))
                        kb.wait("scalar", "VE", t_ep[(oc, tcc)])
                        t_ost_st[(oc, tcc)] = dma(
                            kb, "scalar", f"SD{tcc}",
                            x_dram[128 * oc:128 * oc + 128, TC * tcc:TC * tcc + TC], ost[tcc][:])
                    kb.pump_bg(34)

        def emit_ffn(src_dram, dst_dram, tag):
            t_dst_last = 0
            with ExitStack() as pf:
                xnf = kb.sb(pf, "xnf" + tag, [128, NIT * TC], BF16)
                t_gu_last = 0
                t_down_last = 0
                t_dep_last = 0
                t_od_last = [0, 0]
                for tcc in range(2):
                    kb.wait("sync", "TE", t_gu_last)
                    kb.wait("sync", "VE", t_dep_last)
                    kb.wait("sync", "SD0", t_od_last[0])
                    kb.wait("sync", "SD1", t_od_last[1])
                    with ExitStack() as lns:
                        t_xn = emit_ln(lns, src_dram, tcc, xnf, 0, vec_war=t_gu_last)
                    kb.pump_bg(90)
                    kb.wait("sync", "VE", t_xn)
                    kb.wait("sync", "SE", ln_prev["sq"])
                    kb.wait("sync", "TE", ln_prev["ssq"])
                    with ExitStack() as gus:
                        hbuf = kb.sb(gus, "h" + tag, [128, NI * TC], BF16)
                        gup = ExitStack()
                        wg = [kb.sb(gup, f"wg{i}" + tag, [128, NIT * 128], BF16) for i in range(2)]
                        wu = [kb.sb(gup, f"wu{i}" + tag, [128, NIT * 128], BF16) for i in range(2)]
                        sil = kb.sb(gup, "sil" + tag, [128, TC], BF16)
                        kb.wait("tensor", "VE", t_xn)
                        t_wg_used = {}
                        t_wu_used = {}
                        t_h = {}
                        t_sil_t = {}
                        for nci in range(NI):
                            pbuf = nci % 2
                            if nci == 0 and tcc == 0:
                                kb.wait("sync", "CC", ag_toks["up"])
                            kb.wait("sync", "TE", t_wg_used.get(nci - 2, 0))
                            t_gl = dma(kb, "sync", f"LW{pbuf}",
                                       wg[pbuf][:].rearrange("p (k n) -> p k n", n=128),
                                       gate_bf[:, 128 * nci:128 * nci + 128]
                                       .rearrange("(k p) n -> p k n", p=128))
                            kb.wait("sync", "TE", t_wu_used.get(nci - 2, 0))
                            t_ul = dma(kb, "sync", f"LU{pbuf}",
                                       wu[pbuf][:].rearrange("p (k n) -> p k n", n=128),
                                       up_bf[:, 128 * nci:128 * nci + 128]
                                       .rearrange("(k p) n -> p k n", p=128))
                            if nci == 0 and tcc == 0:
                                kb.wait("tensor", "CC", ag_toks["gate"])
                                kb.wait("tensor", "CC", ag_toks["up"])
                            kb.wait("tensor", f"LW{pbuf}", t_gl)
                            pg = pb[0] if pbuf == 0 else pb[1]
                            pu = pb[2] if pbuf == 0 else pb[3]
                            kb.wait("tensor", "SE", t_sil_t.get(nci - 2, 0))
                            kb.wait("tensor", "VE", t_h.get(nci - 2, 0))
                            t_gmm = None
                            for it in range(NIT):
                                t_gmm = inc(kb, "tensor", "TE",
                                            lambda e, it=it, pbuf=pbuf, pg=pg, wg=wg: e.matmul(
                                                pg[:, :], wg[pbuf][:, 128 * it:128 * it + 128],
                                                xnf[:, TC * it:TC * it + TC],
                                                start=(it == 0), stop=(it == NIT - 1)))
                            t_wg_used[nci] = t_gmm
                            kb.wait("tensor", f"LU{pbuf}", t_ul)
                            t_umm = None
                            for it in range(NIT):
                                t_umm = inc(kb, "tensor", "TE",
                                            lambda e, it=it, pbuf=pbuf, pu=pu, wu=wu: e.matmul(
                                                pu[:, :], wu[pbuf][:, 128 * it:128 * it + 128],
                                                xnf[:, TC * it:TC * it + TC],
                                                start=(it == 0), stop=(it == NIT - 1)))
                            t_wu_used[nci] = t_umm
                            kb.wait("scalar", "TE", t_gmm)
                            kb.wait("scalar", "VE", t_h.get(nci - 1, 0))
                            inc(kb, "scalar", "SE",
                                lambda e, pg=pg, sil=sil: e.activation(sil[:], pg[:, :], AF.Silu))
                            t_sil_t[nci] = inc(kb, "scalar", "SE",
                                               lambda e, sil=sil: e.activation(
                                                   scal1[0:1, 2:4], sil[0:1, TC - 2:TC], AF.Copy))
                            kb.wait("vector", "SE", t_sil_t[nci])
                            kb.wait("vector", "TE", t_umm)
                            if nci == 0:
                                kb.wait("vector", "TE", t_down_last)
                            t_h[nci] = inc(kb, "vector", "VE",
                                           lambda e, nci=nci, pu=pu, hbuf=hbuf, sil=sil: e.tensor_tensor(
                                               hbuf[:, TC * nci:TC * nci + TC], sil[:], pu[:, :],
                                               ALU.mult))
                            kb.pump_bg(4)
                        t_h_all = t_h[NI - 1]
                        kb.pump_bg(400)
                        gup.close()
                        kb.wait("sync", "TE", t_wu_used[NI - 1])
                        kb.wait("sync", "VE", t_h_all)
                        wd = [kb.sb(gus, f"wd{i}" + tag, [128, NI * 128], BF16) for i in range(2)]
                        xr2 = [kb.sb(gus, f"xr2{i}" + tag, [128, TC], F32) for i in range(2)]
                        od = [kb.sb(gus, f"od{i}" + tag, [128, TC], F32) for i in range(2)]
                        t_wd_used = {}
                        t_dep = {}
                        t_od_st = {}
                        for oci in range(NIT):
                            pbuf = oci % 2
                            rk = oci // 4
                            if oci == 0 and tcc == 0:
                                kb.wait("sync", "CC", ag_toks["down"])
                            kb.wait("sync", "TE", t_wd_used.get(oci - 2, 0))
                            t_dl = dma(kb, "sync", f"LDN{pbuf}",
                                       wd[pbuf][:].rearrange("p (k n) -> p k n", n=128),
                                       down_bf[INTER * rk:INTER * rk + INTER,
                                               128 * (oci % 4):128 * (oci % 4) + 128]
                                       .rearrange("(k p) n -> p k n", p=128))
                            kb.wait("sync", "VE", t_dep.get(oci - 2, 0))
                            t_xr = dma(kb, "sync", f"LX{pbuf}", xr2[pbuf][:],
                                       src_dram[128 * oci:128 * oci + 128, TC * tcc:TC * tcc + TC])
                            if oci == 0 and tcc == 0:
                                kb.wait("tensor", "CC", ag_toks["down"])
                            if oci == 0:
                                kb.wait("tensor", "VE", t_h_all)
                            kb.wait("tensor", f"LDN{pbuf}", t_dl)
                            pd = pb[4] if pbuf == 0 else pb[5]
                            kb.wait("tensor", "VE", t_dep.get(oci - 2, 0))
                            t_dmm = None
                            for kt in range(NI):
                                t_dmm = inc(kb, "tensor", "TE",
                                            lambda e, kt=kt, pbuf=pbuf, pd=pd, wd=wd, hbuf=hbuf: e.matmul(
                                                pd[:, :], wd[pbuf][:, 128 * kt:128 * kt + 128],
                                                hbuf[:, TC * kt:TC * kt + TC],
                                                start=(kt == 0), stop=(kt == NI - 1)))
                            t_wd_used[oci] = t_dmm
                            kb.wait("vector", "TE", t_dmm)
                            kb.wait("vector", f"LX{pbuf}", t_xr)
                            kb.wait("vector", f"SD{pbuf}", t_od_st.get((tcc, oci - 2), 0))
                            t_dep[oci] = inc(kb, "vector", "VE",
                                             lambda e, pbuf=pbuf, pd=pd, od=od, xr2=xr2: e.tensor_add(
                                                 od[pbuf][:], xr2[pbuf][:], pd[:, :]))
                            kb.wait("scalar", "VE", t_dep[oci])
                            t_od_st[(tcc, oci)] = dma(kb, "scalar", f"SD{pbuf}",
                                                      dst_dram[128 * oci:128 * oci + 128,
                                                               TC * tcc:TC * tcc + TC],
                                                      od[pbuf][:])
                            t_dst_last = t_od_st[(tcc, oci)]
                            kb.pump_bg(8)
                        t_down_last = t_wd_used[NIT - 1]
                        t_gu_last = kb.cnt["TE"]
                        t_dep_last = t_dep[NIT - 1]
                        t_od_last = [t_od_st[(tcc, NIT - 2)], t_od_st[(tcc, NIT - 1)]]
                        if (NIT - 1) % 2 == 0:
                            t_od_last = [t_od_st[(tcc, NIT - 1)], t_od_st[(tcc, NIT - 2)]]
            return t_dst_last

        if nphase >= 2:
            kb.barrier(exclude=("CC", "GD", "BG"), flush_bg=False)
            emit_ffn(x_dram, x_dram, "f1")
            kb.pump_bg(10 ** 9)
            if kdbg:
                kb.wait("gpsimd", "CC", ag_toks["down"])
                dma(kb, "gpsimd", "GD", dbg_g[:], down_bf[:])
                dma(kb, "gpsimd", "GD", dbg_z[:], zd_scr[:])
                dma(kb, "gpsimd", "GD", dbg_zg[:], zg_scr[:])

        kb.barrier()
        p0.close()

        if nphase >= 2.25:
            with ExitStack() as p3a:
                xn3 = kb.sb(p3a, "xn3", [128, NIT * TC], BF16)
                t_prev_st = 0
                for tcc in range(2):
                    kb.wait("vector", "GD", t_prev_st)
                    with ExitStack() as lns:
                        t_xn = emit_ln(lns, x_dram, tcc, xn3, 0)
                    kb.wait("gpsimd", "VE", t_xn)
                    t_prev_st = dma(kb, "gpsimd", "GD",
                                    xn_contrib[tcc][:, :]
                                    .rearrange("(i p) t -> p i t", p=128),
                                    xn3[:].rearrange("p (i t) -> p i t", t=TC))
                    kb.wait("gpsimd", "GD", t_prev_st)
                    ag_toks[f"xn{tcc}"] = inc(kb, "gpsimd", "CC",
                                              lambda e, tcc=tcc: e.collective_compute(
                                                  "AllGather", ALU.bypass, replica_groups=RG,
                                                  ins=[xn_contrib[tcc].ap().opt()],
                                                  outs=[xn_ag[tcc].ap().opt()]))

            kb.barrier()
        if nphase >= 2.5:
            with ExitStack() as p3b:
                cosq = kb.sb(p3b, "cosq", [128, S], F32)
                sinq = kb.sb(p3b, "sinq", [128, S], F32)
                cosk = kb.sb(p3b, "cosk", [128, S], F32)
                sink = kb.sb(p3b, "sink", [128, S], F32)
                t_tbl = 0
                for dst, src in ((cosq, c_cosq), (sinq, c_sinq), (cosk, c_cosk), (sink, c_sink)):
                    t_tbl = dma(kb, "sync", "LD", dst[:], src[:])
                xsec = [kb.sb(p3b, f"xsec{i}", [128, 16 * 1024], BF16) for i in range(2)]
                wpb = [kb.sb(p3b, f"wpb{i}", [128, NIT * 128], BF16) for i in range(2)]
                ptmp = kb.sb(p3b, "ptmp", [128, 1024], BF16)
                rtmp1 = kb.sb(p3b, "rtmp1", [128, 1024], F32)
                rtmp2 = kb.sb(p3b, "rtmp2", [128, 1024], F32)
                rout = kb.sb(p3b, "rout", [128, 1024], BF16)
                kb.wait("sync", "CC", ag_toks["xn1"])
                kb.wait("sync", "GD", t_qkvw_done)
                kb.wait("tensor", "LD", t_tbl)
                t_xs_used = {}
                t_xs_ld = {}
                t_w_used = {}
                t_ev = {}
                t_ptmp_ve = 0
                t_ptmp_gd = 0
                t_pb45_ve = 0
                t_rout_gd = 0
                for sec in range(8):
                    bs, qb = sec // 4, sec % 4
                    for hf in range(2):
                        kb.wait("sync", "TE", t_xs_used.get((sec - 1, hf), 0))
                        dma(kb, "sync", f"LX{hf}",
                            xsec[hf][:].rearrange("p (i t) -> p (i t)" if False else "p (i t) -> p i t", t=1024)[:, :, 0:TC],
                            xn_ag[0][D * sec + 2048 * hf: D * sec + 2048 * hf + 2048, :]
                            .rearrange("(i p) t -> p i t", p=128))
                        t_xs_ld[(sec, hf)] = dma(
                            kb, "sync", f"LX{hf}",
                            xsec[hf][:].rearrange("p (i t) -> p i t", t=1024)[:, :, TC:1024],
                            xn_ag[1][D * sec + 2048 * hf: D * sec + 2048 * hf + 2048, :]
                            .rearrange("(i p) t -> p i t", p=128))
                    for ock in range(6):
                        wbuf = wpb[ock % 2]
                        wpar = ock % 2
                        kb.wait("sync", "TE", t_w_used.get(6 * sec + ock - 2, 0))
                        if ock < 4:
                            src_w = wq_bf[:, 128 * ock:128 * ock + 128]
                        elif ock == 4:
                            src_w = wk_bf[:, :]
                        else:
                            src_w = wv_bf[:, :]
                        t_wl = dma(kb, "sync", f"LW{wpar}",
                                   wbuf[:].rearrange("p (k n) -> p k n", n=128),
                                   src_w.rearrange("(k p) n -> p k n", p=128))
                        kb.wait("tensor", f"LW{wpar}", t_wl)
                        t_mm_last = None
                        for tq in range(2):
                            pbank = pb[2 * (ock % 2) + tq]
                            kb.wait("tensor", "SE", t_ev.get(2 * (ock % 2) + tq, 0))
                            for hf in range(2):
                                kb.wait("tensor", f"LX{hf}", t_xs_ld[(sec, hf)])
                                for itl in range(16):
                                    first = (hf == 0 and itl == 0)
                                    last = (hf == 1 and itl == 15)
                                    t_mm_last = inc(kb, "tensor", "TE",
                                                    lambda e, hf=hf, itl=itl, tq=tq,
                                                    pbank=pbank, wbuf=wbuf,
                                                    first=first, last=last: e.matmul(
                                                        pbank[:, :],
                                                        wbuf[:, 128 * (16 * hf + itl):128 * (16 * hf + itl) + 128],
                                                        xsec[hf][:, 1024 * itl + 512 * tq:1024 * itl + 512 * tq + 512],
                                                        start=first, stop=last))
                        t_w_used[6 * sec + ock] = t_mm_last
                        t_xs_used[(sec, 0)] = t_mm_last
                        t_xs_used[(sec, 1)] = t_mm_last
                        for tq in range(2):
                            pbank = pb[2 * (ock % 2) + tq]
                            kb.wait("scalar", "TE", t_mm_last)
                            kb.wait("scalar", "VE", t_ptmp_ve)
                            kb.wait("scalar", "GD", t_ptmp_gd)
                            t_ev[2 * (ock % 2) + tq] = inc(
                                kb, "scalar", "SE",
                                lambda e, tq=tq, pbank=pbank: e.activation(
                                    ptmp[:, 512 * tq:512 * tq + 512], pbank[:, :], AF.Copy))
                        t_evd = max(t_ev[2 * (ock % 2)], t_ev[2 * (ock % 2) + 1])
                        if ock == 5:
                            kb.wait("gpsimd", "SE", t_evd)
                            t_ptmp_gd = dma(kb, "gpsimd", "GD",
                                            v_scr[128 * bs:128 * bs + 128,
                                                  1024 * qb:1024 * qb + 1024], ptmp[:])
                        else:
                            cost = cosq if ock < 4 else cosk
                            sint = sinq if ock < 4 else sink
                            kb.wait("tensor", "SE", t_evd)
                            kb.wait("tensor", "VE", t_pb45_ve)
                            t_rt = None
                            for tq in range(2):
                                t_rt = inc(kb, "tensor", "TE",
                                           lambda e, tq=tq: e.matmul(
                                               (pb[4] if tq == 0 else pb[5])[:, :],
                                               rott[:], ptmp[:, 512 * tq:512 * tq + 512],
                                               start=True, stop=True))
                            kb.wait("vector", "SE", t_evd)
                            op(kb, "vector", lambda e, qb=qb, cost=cost: e.tensor_tensor(
                                rtmp1[:], ptmp[:], cost[:, 1024 * qb:1024 * qb + 1024], ALU.mult))
                            t_ptmp_ve = kb.cnt["VE"] + 1
                            kb.wait("vector", "TE", t_rt)
                            op(kb, "vector", lambda e, qb=qb, sint=sint: e.tensor_tensor(
                                rtmp2[:, 0:512], pb[4][:, :],
                                sint[:, 1024 * qb:1024 * qb + 512], ALU.mult))
                            op(kb, "vector", lambda e, qb=qb, sint=sint: e.tensor_tensor(
                                rtmp2[:, 512:1024], pb[5][:, :],
                                sint[:, 1024 * qb + 512:1024 * qb + 1024], ALU.mult))
                            kb.wait("vector", "GD", t_rout_gd)
                            t_rp = inc(kb, "vector", "VE",
                                       lambda e: e.tensor_add(rout[:], rtmp1[:], rtmp2[:]))
                            t_pb45_ve = t_rp
                            assert t_ptmp_ve == t_rp
                            kb.wait("gpsimd", "VE", t_rp)
                            if ock < 4:
                                dstap = q_scr[128 * (4 * bs + ock):128 * (4 * bs + ock) + 128,
                                              1024 * qb:1024 * qb + 1024]
                            else:
                                dstap = k_scr[128 * bs:128 * bs + 128,
                                              1024 * qb:1024 * qb + 1024]
                            t_rout_gd = dma(kb, "gpsimd", "GD", dstap, rout[:])
                t_qkv_st_all = kb.cnt["GD"]

            kb.barrier()
        if nphase >= 2.75:
            with ExitStack() as p3c:
                qsb = [kb.sb(p3c, f"qsb{i}", [128, S], BF16) for i in range(2)]
                ksb = [kb.sb(p3c, f"ksb{i}", [128, S], BF16) for i in range(2)]
                vtb = [kb.sb(p3c, f"vtb{i}", [128, S], BF16) for i in range(2)]
                vnat = [kb.sb(p3c, f"vnat{i}", [128, NIT * 128], BF16) for i in range(2)]
                srow = kb.sb(p3c, "srow", [128, S], F32)
                atn = kb.sb(p3c, "atn", [128, S], BF16)
                atT = [kb.sb(p3c, f"atT{i}", [128, 512], BF16) for i in range(2)]
                ctxev = kb.sb(p3c, "ctxev", [128, 512], BF16)
                kb.wait("sync", "GD", t_qkv_st_all)
                t_b_used = {}
                t_vn = {}
                t_sc_ev = {}
                t_exp_prev = 0
                t_tr_last_prev = 0
                t_at_used = {}
                t_ctx_st = 0
                t_ctx_ev_tok = 0
                for bh in range(8):
                    b_, h_ = bh // 4, bh % 4
                    par = bh % 2
                    kb.wait("sync", "TE", t_b_used.get(bh - 2, 0))
                    t_ql = dma(kb, "sync", f"LA{par}", qsb[par][:],
                               q_scr[128 * (4 * b_ + h_):128 * (4 * b_ + h_) + 128, :])
                    t_kl = dma(kb, "sync", f"LA{par}", ksb[par][:],
                               k_scr[128 * b_:128 * b_ + 128, :])
                    t_vl = dma(kb, "sync", f"LA{par}", vtb[par][:],
                               v_scr[128 * b_:128 * b_ + 128, :])
                    kb.wait("tensor", f"LA{par}", t_vl)
                    t_ev_v = t_vn.get(bh - 2, 0)
                    for g4 in range(8):
                        kb.wait("tensor", "VE", t_ev_v)
                        t_tp = None
                        for k4 in range(4):
                            it = 4 * g4 + k4
                            t_tp = inc(kb, "tensor", "TE",
                                       lambda e, it=it, k4=k4, par=par: e.matmul(
                                           pb[2][:, 128 * k4:128 * k4 + 128],
                                           vtb[par][:, 128 * it:128 * it + 128], ident[:],
                                           start=True, stop=True))
                        kb.wait("vector", "TE", t_tp)
                        t_ev_v = inc(kb, "vector", "VE",
                                     lambda e, g4=g4, par=par: e.tensor_copy(
                                         vnat[par][:, 512 * g4:512 * g4 + 512], pb[2][:, :]))
                    t_vn[bh] = t_ev_v
                    kb.wait("tensor", f"LA{par}", max(t_ql, t_kl))
                    for t in range(32):
                        KL = 128 * (t + 1)
                        nch = (KL + 511) // 512
                        t_sc_mm = {}
                        t_evs = []
                        for ch in range(nch):
                            w = min(512, KL - 512 * ch)
                            pbank = pb[ch % 2]
                            kb.wait("tensor", "VE", t_sc_ev.get(("v", ch % 2), 0))
                            kb.wait("tensor", "SE", t_sc_ev.get(("s", ch % 2), 0))
                            t_sc_mm[ch] = inc(kb, "tensor", "TE",
                                              lambda e, t=t, ch=ch, w=w, par=par, pbank=pbank: e.matmul(
                                                  pbank[:, 0:w],
                                                  qsb[par][:, 128 * t:128 * t + 128],
                                                  ksb[par][:, 512 * ch:512 * ch + w],
                                                  start=True, stop=True))
                            last = (ch == nch - 1)
                            if last:
                                if w > 128:
                                    kb.wait("scalar", "TE", t_sc_mm[ch])
                                    kb.wait("scalar", "SE", t_exp_prev)
                                    kb.wait("scalar", "VE", t_sc_ev.get(("v", ch % 2), 0))
                                    t_e1 = inc(kb, "scalar", "SE",
                                               lambda e, ch=ch, w=w, pbank=pbank: e.copy(
                                                   srow[:, 512 * ch:512 * ch + w - 128],
                                                   pbank[:, 0:w - 128]))
                                    t_sc_ev[("s", ch % 2)] = t_e1
                                    t_evs.append(("SE", t_e1))
                                kb.wait("vector", "TE", t_sc_mm[ch])
                                kb.wait("vector", "SE", t_exp_prev)
                                kb.wait("vector", "SE", t_sc_ev.get(("s", ch % 2), 0))
                                t_e2 = inc(kb, "vector", "VE",
                                           lambda e, ch=ch, w=w, pbank=pbank: e.tensor_add(
                                               srow[:, 512 * ch + w - 128:512 * ch + w],
                                               pbank[:, w - 128:w], maskt[:]))
                                t_sc_ev[("v", ch % 2)] = t_e2
                                t_evs.append(("VE", t_e2))
                            else:
                                kb.wait("scalar", "TE", t_sc_mm[ch])
                                kb.wait("scalar", "SE", t_exp_prev)
                                kb.wait("scalar", "VE", t_sc_ev.get(("v", ch % 2), 0))
                                t_e1 = inc(kb, "scalar", "SE",
                                           lambda e, ch=ch, pbank=pbank: e.copy(
                                               srow[:, 512 * ch:512 * ch + 512], pbank[:, :]))
                                t_sc_ev[("s", ch % 2)] = t_e1
                                t_evs.append(("SE", t_e1))
                        for sname, tv in t_evs:
                            kb.wait("vector", sname, tv)
                        t_max = inc(kb, "vector", "VE",
                                    lambda e, KL=KL: e.reduce_max(nmax[:], srow[:, 0:KL],
                                                                  axis=AX.X, negate=True))
                        kb.wait("scalar", "VE", t_max)
                        kb.wait("scalar", "TE", t_tr_last_prev)
                        t_exp = inc(kb, "scalar", "SE",
                                    lambda e, KL=KL: e.activation(
                                        atn[:, 0:KL], srow[:, 0:KL], AF.Exp,
                                        bias=nmax[:, 0:1], accum_out=rsum[:]))
                        t_exp_prev = t_exp
                        t_exp = inc(kb, "scalar", "SE",
                                    lambda e, KL=KL: e.activation(
                                        scal1[0:1, 0:2], atn[0:1, KL - 2:KL], AF.Square,
                                        bias=rsum[0:1, 0:1]))
                        t_exp_prev = t_exp
                        kb.wait("vector", "SE", t_exp)
                        op(kb, "vector", lambda e: e.reciprocal(rinv[:], rsum[:]))
                        kb.wait("vector", "TE", t_tr_last_prev)
                        t_dm = inc(kb, "vector", "VE",
                                   lambda e: e.tensor_scalar_mul(dmt[:], ident[:], rinv[:, 0:1]))
                        pc = pb[4]
                        kb.wait("tensor", "VE", t_dm)
                        nkt = t + 1
                        ngrp = (nkt + 3) // 4
                        grp_ev = {}

                        def emit_trans(kg):
                            tpar = kg % 2
                            ptr = pb[2] if tpar == 0 else pb[3]
                            n_in_g = min(4, nkt - 4 * kg)
                            kb.wait("tensor", "VE", t_at_used.get(("ev", tpar), 0))
                            t_tp = None
                            for k4 in range(n_in_g):
                                kt = 4 * kg + k4
                                t_tp = inc(kb, "tensor", "TE",
                                           lambda e, kt=kt, k4=k4, ptr=ptr: e.matmul(
                                               ptr[:, 128 * k4:128 * k4 + 128],
                                               atn[:, 128 * kt:128 * kt + 128], dmt[:],
                                               start=True, stop=True))
                            kb.wait("vector", "TE", t_tp)
                            kb.wait("vector", "TE", t_at_used.get(("mm", tpar), 0))
                            grp_ev[kg] = inc(kb, "vector", "VE",
                                             lambda e, tpar=tpar, n=n_in_g: e.tensor_copy(
                                                 atT[tpar][:, 0:128 * n],
                                                 (pb[2] if tpar == 0 else pb[3])[:, 0:128 * n]))
                            t_at_used[("ev", tpar)] = grp_ev[kg]

                        def emit_ctx(kg):
                            tpar = kg % 2
                            n_in_g = min(4, nkt - 4 * kg)
                            kb.wait("tensor", "VE", grp_ev[kg])
                            if kg == 0:
                                kb.wait("tensor", "SE", t_ctx_ev_tok)
                            t_cmm = None
                            for k4 in range(n_in_g):
                                kt = 4 * kg + k4
                                t_cmm = inc(kb, "tensor", "TE",
                                            lambda e, kt=kt, k4=k4, par=par, t=t, tpar=tpar,
                                            first=(kt == 0), last=(kt == t): e.matmul(
                                                pc[:, 128 * (t % 4):128 * (t % 4) + 128],
                                                vnat[par][:, 128 * kt:128 * kt + 128],
                                                atT[tpar][:, 128 * k4:128 * k4 + 128],
                                                start=first, stop=last))
                            t_at_used[("mm", tpar)] = t_cmm

                        for kg in range(ngrp):
                            emit_trans(kg)
                            if kg > 0:
                                emit_ctx(kg - 1)
                        emit_ctx(ngrp - 1)
                        t_tr_last_prev = kb.cnt["TE"]
                        if t % 4 == 3:
                            kb.wait("scalar", "TE", t_tr_last_prev)
                            kb.wait("scalar", "GD", t_ctx_st)
                            t_ctx_ev_tok = inc(kb, "scalar", "SE",
                                               lambda e: e.copy(ctxev[:], pc[:, :]))
                            kb.wait("gpsimd", "SE", t_ctx_ev_tok)
                            t_ctx_st = dma(kb, "gpsimd", "GD",
                                           ctx_contrib[512 * b_ + 128 * h_:512 * b_ + 128 * h_ + 128,
                                                       512 * (t // 4):512 * (t // 4) + 512],
                                           ctxev[:])
                    t_b_used[bh] = kb.cnt["TE"]
                kb.wait("gpsimd", "GD", t_ctx_st)
                ag_toks["ctx"] = inc(kb, "gpsimd", "CC",
                                     lambda e: e.collective_compute(
                                         "AllGather", ALU.bypass, replica_groups=RG,
                                         ins=[ctx_contrib.ap().opt()], outs=[ctx_ag.ap().opt()]))

            kb.barrier()
        if nphase >= 3:
            with ExitStack() as p3d:
                ctxT = kb.sb(p3d, "ctxT", [128, NIT * 1024], BF16)
                wob = [kb.sb(p3d, f"wob{i}", [128, NIT * 128], BF16) for i in range(2)]
                xr3 = [kb.sb(p3d, f"xr3{i}", [128, TC], F32) for i in range(2)]
                od3 = [kb.sb(p3d, f"od3{i}", [128, TC], F32) for i in range(2)]
                cell = {}
                kb.wait("gpsimd", "LD", t_const)
                op(kb, "gpsimd", lambda e: cell.__setitem__("bv", nc.values_load(
                    ridb[0:1, 0:1], engines=[ET.Pool], min_val=0, max_val=1,
                    skip_runtime_bounds_check=True)))
                op(kb, "gpsimd", lambda e: cell.__setitem__("qv", nc.values_load(
                    ridq[0:1, 0:1], engines=[ET.Pool], min_val=0, max_val=3,
                    skip_runtime_bounds_check=True)))
                kb.wait("gpsimd", "CC", ag_toks["ctx"])
                view2 = ctx_ag.rearrange("(blk two hh p) (qc t) -> p blk hh two qc t",
                                         two=2, hh=4, p=128, qc=4)
                h_gd = kb.sems["GD"]
                t_ctl = 0
                for blk in range(8):
                    t_ctl = kb.tok("GD", 16)
                    out_ap = ctxT[:].rearrange("p (blk hh t) -> p blk hh t",
                                               hh=4, t=1024)[:, blk:blk + 1, :, :]

                    def dyn_load(e, blk=blk, h_gd=h_gd, out_ap=out_ap):
                        src = view2[:, blk:blk + 1, :, bass.ts(cell["bv"], 1),
                                    bass.ts(cell["qv"], 1), :]
                        return e.dma_start(out=out_ap, in_=src).then_inc(h_gd, 16)
                    kb.emit("gpsimd", dyn_load)
                if kdbg:
                    kb.wait("gpsimd", "GD", t_ctl)
                    dma(kb, "gpsimd", "GD", dbg_ct[:], ctxT[:])
                kb.wait("tensor", "GD", t_ctl)
                kb.wait("tensor", "CC", ag_toks["wo"])
                t_w_used3 = {}
                t_ep3 = {}
                t_ost3 = {}
                for oci in range(NIT):
                    pbuf = oci % 2
                    if oci == 0:
                        kb.wait("sync", "CC", ag_toks["wo"])
                    kb.wait("sync", "TE", t_w_used3.get(oci - 2, 0))
                    t_wl = dma(kb, "sync", f"LW{pbuf}",
                               wob[pbuf][:].rearrange("p (k n) -> p k n", n=128),
                               wo_bf[:, 128 * oci:128 * oci + 128]
                               .rearrange("(k p) n -> p k n", p=128))
                    kb.wait("tensor", f"LW{pbuf}", t_wl)
                    t_mm2 = {}
                    for tcc in range(2):
                        pbank = pb[tcc]
                        kb.wait("tensor", "VE", t_ep3.get((oci - 1, tcc), 0))
                        t_mm = None
                        for it in range(NIT):
                            t_mm = inc(kb, "tensor", "TE",
                                       lambda e, it=it, tcc=tcc, pbuf=pbuf, pbank=pbank: e.matmul(
                                           pbank[:, :], wob[pbuf][:, 128 * it:128 * it + 128],
                                           ctxT[:, 1024 * it + 512 * tcc:1024 * it + 512 * tcc + 512],
                                           start=(it == 0), stop=(it == NIT - 1)))
                        t_mm2[tcc] = t_mm
                    t_w_used3[oci] = t_mm2[1]
                    for tcc in range(2):
                        pbank = pb[tcc]
                        kb.wait("sync", "VE", t_ep3.get((oci - 1, tcc), 0))
                        t_xr = dma(kb, "sync", f"LX{tcc}", xr3[tcc][:],
                                   x_dram[128 * oci:128 * oci + 128, TC * tcc:TC * tcc + TC])
                        kb.wait("vector", "TE", t_mm2[tcc])
                        kb.wait("vector", f"LX{tcc}", t_xr)
                        kb.wait("vector", f"SD{tcc}", t_ost3.get((oci - 1, tcc), 0))
                        t_ep3[(oci, tcc)] = inc(kb, "vector", "VE",
                                                lambda e, tcc=tcc, pbank=pbank: e.tensor_add(
                                                    od3[tcc][:], xr3[tcc][:], pbank[:, :]))
                        kb.wait("scalar", "VE", t_ep3[(oci, tcc)])
                        t_ost3[(oci, tcc)] = dma(kb, "scalar", f"SD{tcc}",
                                                 x_dram[128 * oci:128 * oci + 128,
                                                        TC * tcc:TC * tcc + TC],
                                                 od3[tcc][:])

        kb.barrier()
        if nphase >= 4:
            emit_ffn(x_dram, out_ext, "f4")
        else:
            with ExitStack() as pf:
                cp = kb.sb(pf, "cp", [128, NIT * TC], F32)
                t_s = 0
                for tcc in range(2):
                    kb.wait("sync", "SD0", t_s)
                    t_l = dma(kb, "sync", "LD", cp[:].rearrange("p (i t) -> p i t", t=TC),
                              x_dram[:, tcc * TC:(tcc + 1) * TC].rearrange("(i p) t -> p i t", p=128))
                    kb.wait("scalar", "LD", t_l)
                    t_s = dma(kb, "scalar", "SD0",
                              out_ext[:, tcc * TC:(tcc + 1) * TC].rearrange("(i p) t -> p i t", p=128),
                              cp[:].rearrange("p (i t) -> p i t", t=TC))
                    kb.wait("scalar", "SD0", t_s)

        kb.wait("sync", "SD0", kb.cnt["SD0"])
        kb.wait("sync", "SD1", kb.cnt["SD1"])
        kb.wait("sync", "GD", kb.cnt["GD"])

        assert not kb.bg_vector, f"{len(kb.bg_vector)} background ops never pumped"

        with nc.Block() as block:
            @block.sync
            def _(e):
                for f in kb.streams["sync"]:
                    f(e)

            @block.tensor
            def _(e):
                for f in kb.streams["tensor"]:
                    f(e)

            @block.vector
            def _(e):
                for f in kb.streams["vector"]:
                    f(e)

            @block.scalar
            def _(e):
                for f in kb.streams["scalar"]:
                    f(e)

            @block.gpsimd
            def _(e):
                for f in kb.streams["gpsimd"]:
                    f(e)

    return nc


def kernel(**inputs):
    in_maps = shard_inputs(inputs)
    if "nc" not in _NC_CACHE:
        _NC_CACHE["nc"] = build_program(4)
    nc = _NC_CACHE["nc"]
    res = run_bass_kernel_spmd(nc, in_maps, core_ids=list(range(NCORE)))
    xt = np.concatenate([np.asarray(res.results[c]["out"]) for c in range(NCORE)], axis=1)
    out = np.ascontiguousarray(xt.T).reshape(2, S, D).astype(np.float32)
    return out
